# version 07
# baseline (speedup 1.0000x reference)
import math
import numpy as np

import concourse.bass as bass
import concourse.mybir as mybir
from concourse.bass_utils import run_bass_kernel_spmd

F32 = mybir.dt.float32
BF16 = mybir.dt.bfloat16
AF = mybir.ActivationFunctionType
ALU = mybir.AluOpType
AX = mybir.AxisListType

NCORE = 8
B_PER_CORE = 2
S = 2048
D = 1024
NH = 16
DH = 64
HM = 1408
NE = 8
NTILE = S // 128
NDC = D // 128
NHC = HM // 128

ENGS = ("T", "V", "S", "SY", "G")



class Op:
    __slots__ = ("eng", "fn", "deps", "dma", "need_inc", "val", "waits", "clock", "i")

    def __init__(self, eng, fn, deps, dma, i):
        self.eng = eng
        self.fn = fn
        self.deps = deps
        self.dma = dma
        self.need_inc = False
        self.val = None
        self.waits = None
        self.clock = None
        self.i = i


class Res:
    __slots__ = ("lw", "readers")

    def __init__(self):
        self.lw = None
        self.readers = []


class Prog:
    def __init__(self, nc):
        self.nc = nc
        self.ops = []

    def op(self, eng, fn, reads=(), writes=(), dma=False):
        deps = []
        for r in reads:
            if r.lw is not None:
                deps.append(r.lw)
        for w in writes:
            deps.extend(w.readers)
            if w.lw is not None:
                deps.append(w.lw)
        o = Op(eng, fn, deps, dma, len(self.ops))
        self.ops.append(o)
        for r in reads:
            r.readers.append(o)
        for w in writes:
            w.lw = o
            w.readers = []
        return o

    @staticmethod
    def _free_dep(d, o):
        return d.eng == o.eng and not d.dma and not o.dma and o.eng == "T"

    def schedule(self):
        for o in self.ops:
            for d in o.deps:
                if not self._free_dep(d, o):
                    d.need_inc = True
        POOL = 8
        counts = {}
        for o in self.ops:
            if o.dma:
                j = counts.get((o.eng, "dj"), 0)
                counts[(o.eng, "dj")] = j + 1
                key = (o.eng, "d%d" % (j % POOL))
                c = counts.get(key, 0) + 16
                counts[key] = c
                o.val = (key, c)
            elif o.need_inc:
                key = (o.eng, "c")
                c = counts.get(key, 0) + 1
                counts[key] = c
                o.val = (key, c)
        marks = {e: {} for e in ENGS}
        for o in self.ops:
            wm = marks[o.eng]
            need = {}
            for d in o.deps:
                if self._free_dep(d, o):
                    continue
                k, v = d.val
                if wm.get(k, 0) < v and need.get(k, 0) < v:
                    need[k] = v
            for d in o.deps:
                if self._free_dep(d, o):
                    continue
                if d.clock:
                    for k, v in d.clock.items():
                        if wm.get(k, 0) < v:
                            wm[k] = v
            for k, v in need.items():
                if wm.get(k, 0) >= v:
                    continue
                wm[k] = v
            o.waits = sorted((k, v) for k, v in need.items() if v > 0)
            o.clock = dict(wm)
        self.counts = counts
        return counts

    def emit(self):
        nc = self.nc
        counts = self.schedule()

        sem_keys = sorted({o.val[0] for o in self.ops if o.val is not None}
                          | {k for o in self.ops for (k, _) in (o.waits or [])})
        sems = {}
        import contextlib
        stack = contextlib.ExitStack()
        for k in sem_keys:
            sems[k] = stack.enter_context(nc.semaphore(f"sem_{k[0]}_{k[1]}"))

        by_eng = {e: [o for o in self.ops if o.eng == e] for e in ENGS}

        def emit_for(eng_obj, eng_name):
            for o in by_eng[eng_name]:
                for k, v in o.waits:
                    eng_obj.wait_ge(sems[k], v)
                ins = o.fn(eng_obj)
                if o.dma:
                    ins.then_inc(sems[o.val[0]], 16)
                elif o.need_inc:
                    ins.then_inc(sems[o.val[0]], 1)
            for s in range(8):
                dk = (eng_name, "d%d" % s)
                if dk in counts and counts[dk] > 0 and dk in sems:
                    eng_obj.wait_ge(sems[dk], counts[dk])

        with nc.Block() as block:
            @block.tensor
            def _(t):
                emit_for(t, "T")

            @block.vector
            def _(v):
                emit_for(v, "V")

            @block.scalar
            def _(s):
                emit_for(s, "S")

            @block.sync
            def _(s):
                emit_for(s, "SY")

            @block.gpsimd
            def _(g):
                emit_for(g, "G")
        stack.close()

    def verify(self):
        idx = {e: 0 for e in ENGS}
        done = [False] * len(self.ops)
        sems = {}
        by_eng = {e: [o for o in self.ops if o.eng == e] for e in ENGS}
        pending = []
        bad = []
        while True:
            progressed = False
            order = sorted(ENGS, key=lambda e: 0 if (
                idx[e] < len(by_eng[e]) and
                any(not done[d.i] for d in by_eng[e][idx[e]].deps)) else 1)
            for e in order:
                if idx[e] >= len(by_eng[e]):
                    continue
                o = by_eng[e][idx[e]]
                if all(sems.get(k, 0) >= v for k, v in (o.waits or [])):
                    for d in o.deps:
                        if not done[d.i]:
                            bad.append((o.i, d.i, o.eng, d.eng))
                    if o.dma:
                        pending.append(o)
                    else:
                        done[o.i] = True
                        if o.val is not None:
                            k, v = o.val
                            sems[k] = sems.get(k, 0) + 1
                    idx[e] += 1
                    progressed = True
                    break
            if not progressed:
                if pending:
                    o = pending.pop(0)
                    done[o.i] = True
                    k, v = o.val
                    sems[k] = sems.get(k, 0) + 16
                else:
                    break
        stuck = [e for e in ENGS if idx[e] < len(by_eng[e])]
        return bad, stuck

    def barrier_guard(self, reslist):
        last = {}
        for r in reslist:
            cand = list(r.readers) + ([r.lw] if r.lw else [])
            for o in cand:
                k = (o.eng, o.dma)
                if k not in last or last[k].i < o.i:
                    last[k] = o
        return list(last.values())


class Arena:

    def __init__(self, prog, ap, cols):
        self.prog = prog
        self.ap = ap
        self.cols = cols
        self.off = 0
        self.live = []
        self.guard = []

    def alloc(self, ncols, nres=1):
        assert self.off + ncols <= self.cols, \
            f"arena overflow: {self.off}+{ncols} > {self.cols}"
        view = self.ap[:, self.off:self.off + ncols]
        self.off += ncols
        rs = [Res() for _ in range(nres)]
        for r in rs:
            r.readers = list(self.guard)
        self.live.extend(rs)
        return (view, rs[0]) if nres == 1 else (view, rs)

    def reset(self):
        g = self.prog.barrier_guard(self.live)
        self.guard = g
        self.live = []
        self.off = 0



def cdiv(a, b):
    return (a + b - 1) // b


class K:

    def __init__(self, name="mk"):
        self.nc = bass.Bass("TRN2", target_bir_lowering=False)
        self.p = Prog(self.nc)
        self._alloc = []
        self.consts = {}

    def sb(self, name, shape, dtype):
        h = self.nc.alloc_sbuf_tensor(name, shape, dtype)
        return h.ap()

    def psum(self, name, shape, dtype):
        h = self.nc.alloc_psum_tensor(name, shape, dtype)
        return h.ap()

    def dram(self, name, shape, dtype, kind="Internal"):
        return self.nc.dram_tensor(name, shape, dtype, kind=kind).ap()

    def const(self, name, arr):
        if name not in self.consts:
            self.consts[name] = self.nc.inline_tensor(arr, name=name).ap()
        return self.consts[name]


def rn(*parts):
    return [Res() for _ in range(parts[0])] if len(parts) == 1 else None



class Core:

    def __init__(self, k: K, n_seq=B_PER_CORE):
        self.k = k
        nc, p = k.nc, k.p
        self.n_seq = n_seq

        self.xt = k.sb("xt", [128, NTILE * D], F32)
        self.xln = k.sb("xln", [128, NDC * S], BF16)
        self.ab = k.sb("ab", [128, 32768], BF16)
        self.af = k.sb("af", [128, 4096], F32)
        self.r_x = [Res() for _ in range(NTILE)]
        self.r_xln = [Res() for _ in range(NDC * NTILE)]
        self.arena = Arena(p, self.ab, 32768)
        self.farena = Arena(p, self.af, 4096)

        self.PS = [k.psum(f"ps{i}", [128, 1024], F32) for i in range(3)]
        self.r_PS = [[Res(), Res()] for _ in range(3)]
        self.TP = [k.psum(f"tp{i}", [128, 512], BF16) for i in range(2)]
        self.r_TP = [Res(), Res()]

        self.ident = k.sb("identt", [128, 128], BF16)
        self.r_ident = Res()
        self.ones_row = k.sb("ones_row", [1, 128], BF16)
        self.r_ones = Res()
        self.smalls = {}
        self.r_smalls = {}

    def tp_slot(self, i):
        t = i % 2
        s = (i // 2) % 4
        return self.TP[t][:, s * 128:(s + 1) * 128], self.r_TP[t]

    def small(self, name, shape, dtype):
        if name not in self.smalls:
            self.smalls[name] = self.k.sb("sm_" + name, shape, dtype)
            self.r_smalls[name] = Res()
        return self.smalls[name], self.r_smalls[name]

    def prep_scratch(self, name, ncols):
        if not hasattr(self, "_prep"):
            self._prep = {}
        if name not in self._prep:
            self._prep[name] = self.farena.alloc(ncols)
        return self._prep[name]


def emit_consts(c: Core):
    k, p = c.k, c.k.p
    import ml_dtypes
    ident_d = k.const("c_ident_bf",
                      np.eye(128, dtype=np.float32).astype(ml_dtypes.bfloat16))
    ones_d = k.const("c_ones_bf", np.ones((1, 128), dtype=np.float32).astype(ml_dtypes.bfloat16))
    p.op("G", lambda g: g.dma_start(out=c.ident[:], in_=ident_d),
         writes=[c.r_ident], dma=True)
    p.op("G", lambda g: g.dma_start(out=c.ones_row[:], in_=ones_d),
         writes=[c.r_ones], dma=True)


def emit_load_x(c: Core, x_in_ap, b):
    p = c.k.p
    src = x_in_ap[b].rearrange("(c p) d -> p c d", p=128)
    dst = c.xt.rearrange("p (c d) -> p c d", d=D)
    for g4 in range(NTILE // 4):
        sl4 = slice(g4 * 4, g4 * 4 + 4)
        p.op("SY", lambda s, sl4=sl4: s.dma_start(out=dst[:, sl4], in_=src[:, sl4]),
             writes=c.r_x[g4 * 4:g4 * 4 + 4], dma=True)


def emit_store_x(c: Core, x_out_ap, b):
    p = c.k.p
    dst = x_out_ap[b].rearrange("(c p) d -> p c d", p=128)
    src = c.xt.rearrange("p (c d) -> p c d", d=D)
    for g4 in range(NTILE // 4):
        sl4 = slice(g4 * 4, g4 * 4 + 4)
        p.op("SY", lambda s, sl4=sl4: s.dma_start(out=dst[:, sl4], in_=src[:, sl4]),
             reads=c.r_x[g4 * 4:g4 * 4 + 4], dma=True)


def emit_col_prep(c: Core, src_ap, n, name, scale=None, psslot=0):
    k, p = c.k, c.k.p
    m = n // 128
    stage, r_stage = c.prep_scratch("colstage", 128)
    col, r_col = c.small(name, [128, m], F32)
    p.op("G", lambda g: g.dma_start(out=stage[0:m, :], in_=src_ap.rearrange("(c p) -> c p", p=128)),
         writes=[r_stage], dma=True)
    ps = c.PS[psslot]
    rps = c.r_PS[psslot][0]
    identf, r_idf = c.prep_scratch("identf", 128)
    if r_idf.lw is None:
        idd = k.const("c_ident_f32", np.eye(128, dtype=np.float32))
        p.op("G", lambda g: g.dma_start(out=identf[:], in_=idd),
             writes=[r_idf], dma=True)
    p.op("T", lambda t: t.transpose(ps[:, 0:128], stage[:], identf[:]),
         reads=[r_stage, r_idf], writes=[rps])
    if scale is None:
        p.op("V", lambda v: v.tensor_copy(col[:], ps[:, 0:m]),
             reads=[rps], writes=[r_col])
    else:
        p.op("V", lambda v: v.tensor_scalar(col[:], ps[:, 0:m], float(scale), None, op0=ALU.mult),
             reads=[rps], writes=[r_col])
    return col, r_col


def emit_row_prep(c: Core, src_ap, n, name, scale=None):
    k, p = c.k, c.k.p
    rf0, r_rf = c.prep_scratch("rowstage", D)
    rf = rf0[0:1, 0:n]
    rb, r_rb = c.small(name, [1, n], BF16)
    p.op("G", lambda g: g.dma_start(out=rf, in_=src_ap[None, :]),
         writes=[r_rf], dma=True)
    sc = 1.0 if scale is None else float(scale)
    p.op("V", lambda v: v.tensor_scalar(rb[:], rf, float(sc), None, op0=ALU.mult),
         reads=[r_rf], writes=[r_rb])
    return rb, r_rb


def emit_prepass_matrix(c: Core, src_ap, dst_dram, scale=None, tpbase=0):
    k, p = c.k, c.k.p
    a, b = src_ap.shape
    ac, bc = a // 128, b // 128
    if not hasattr(c, "_prepbufs"):
        w0, r0 = c.arena.alloc(4 * NDC * D, nres=4)
        c._prepbufs = [(w0[:, i * NDC * D:(i + 1) * NDC * D], r0[i]) for i in range(4)]
        c._prep_i = 0
    wsrc, r_src = c._prepbufs[2 * (c._prep_i % 2)]
    wdst, r_dst = c._prepbufs[2 * (c._prep_i % 2) + 1]
    c._prep_i += 1
    p.op("G", lambda g: g.dma_start(
        out=wsrc.rearrange("p (rc n) -> p rc n", n=b), 
        in_=src_ap.rearrange("(rc p) n -> p rc n", p=128)),
        writes=[r_src], dma=True)
    cnt = 0
    for i in range(ac):
        for j in range(bc):
            tp, rtp = c.tp_slot(tpbase + cnt)
            cnt += 1
            sl_in = wsrc[:, i * b + j * 128: i * b + j * 128 + 128]
            sl_out = wdst[:, j * a + i * 128: j * a + i * 128 + 128]
            p.op("T", lambda t, si=sl_in, tpx=tp: t.transpose(tpx, si, c.ident[:]),
                 reads=[r_src, c.r_ident], writes=[rtp])
            eng = "V" if (cnt % 2 == 0) else "S"
            if scale is None:
                if eng == "V":
                    p.op("V", lambda v, so=sl_out, tpx=tp: v.tensor_copy(so, tpx),
                         reads=[rtp], writes=[r_dst])
                else:
                    p.op("S", lambda s, so=sl_out, tpx=tp: s.copy(so, tpx),
                         reads=[rtp], writes=[r_dst])
            else:
                p.op("V", lambda v, so=sl_out, tpx=tp: v.tensor_scalar(so, tpx, float(scale), None, op0=ALU.mult),
                     reads=[rtp], writes=[r_dst])
    p.op("SY", lambda s: s.dma_start(
        out=dst_dram.rearrange("(bch p) m -> p bch m", p=128),
        in_=wdst.rearrange("p (bch m) -> p bch m", m=a)),
        reads=[r_dst], dma=True)
    return cnt



def emit_ln(c: Core, need_err=False, tpbase=0):
    k, p = c.k, c.k.p
    c.arena.reset()
    c.farena.reset()
    xsq1, r_xsq1 = c.farena.alloc(1024)
    xsq2, r_xsq2 = c.farena.alloc(1024)
    xsqs, r_xsqs = [xsq1, xsq2], [r_xsq1, r_xsq2]
    nf, r_nf = c.farena.alloc(1024)
    scr, r_scrs = c.arena.alloc(2 * 1024, nres=2)
    scrs = [scr[:, 0:1024], scr[:, 1024:2048]]
    if need_err:
        escr, r_escrs = c.arena.alloc(2 * 1024, nres=2)
        escrs = [escr[:, 0:1024], escr[:, 1024:2048]]
        xerr, r_xerr = c.arena.alloc(NDC * S, nres=NDC * NTILE)
    else:
        xerr, r_xerr = None, None

    st, r_st = c.small("lnstat", [128, 9 * 16], F32)
    sums, ssq, mcol, ex2, var, rstd, msq, vare, lnv = (
        st[:, i * 16:(i + 1) * 16] for i in range(9))
    inv = 1.0 / D

    tp_i = tpbase
    for g4 in range(NTILE // 4):
        cg = slice(g4 * 4, g4 * 4 + 4)
        for cc in range(g4 * 4, g4 * 4 + 4):
            vx = c.xt[:, cc * D:(cc + 1) * D]
            sl = cc % 2
            p.op("V", lambda v, vx=vx, cc=cc: v.reduce_sum(
                sums[:, cc:cc + 1], vx, axis=AX.X),
                reads=[c.r_x[cc]], writes=[r_st])
            p.op("S", lambda s, vx=vx, sl=sl, cc=cc: s.activation(
                xsqs[sl][:], vx, AF.Square, accum_out=ssq[:, cc:cc + 1]),
                reads=[c.r_x[cc]], writes=[r_xsqs[sl], r_st])
        p.op("V", lambda v, cg=cg: v.tensor_scalar(mcol[:, cg], sums[:, cg], inv, None, op0=ALU.mult),
             reads=[r_st], writes=[r_st])
        p.op("V", lambda v, cg=cg: v.tensor_scalar(ex2[:, cg], ssq[:, cg], inv, None, op0=ALU.mult),
             reads=[r_st], writes=[r_st])
        p.op("V", lambda v, cg=cg: v.tensor_tensor(msq[:, cg], mcol[:, cg], mcol[:, cg], op=ALU.mult),
             reads=[r_st], writes=[r_st])
        p.op("V", lambda v, cg=cg: v.tensor_tensor(var[:, cg], ex2[:, cg], msq[:, cg], op=ALU.subtract),
             reads=[r_st], writes=[r_st])
        p.op("V", lambda v, cg=cg: v.tensor_scalar(vare[:, cg], var[:, cg], 1e-5, None, op0=ALU.add),
             reads=[r_st], writes=[r_st])
        p.op("S", lambda s, cg=cg: s.activation(lnv[:, cg], vare[:, cg], AF.Sqrt),
             reads=[r_st], writes=[r_st])
        p.op("V", lambda v, cg=cg: v.reciprocal(rstd[:, cg], lnv[:, cg]),
             reads=[r_st], writes=[r_st])

        for cc in range(g4 * 4, g4 * 4 + 4):
            vx = c.xt[:, cc * D:(cc + 1) * D]
            sl = cc % 2
            rbc = rstd[:, cc:cc + 1].broadcast_to([128, D])
            if need_err:
                p.op("V", lambda v, vx=vx, cc=cc, rbc=rbc: v.scalar_tensor_tensor(
                    nf[:], vx, mcol[:, cc:cc + 1], rbc, op0=ALU.subtract, op1=ALU.mult),
                    reads=[c.r_x[cc], r_st], writes=[r_nf])
                p.op("V", lambda v, sl=sl: v.tensor_copy(scrs[sl][:], nf[:]),
                     reads=[r_nf], writes=[r_scrs[sl]])
                p.op("V", lambda v, sl=sl: v.tensor_tensor(escrs[sl][:], nf[:], scrs[sl][:], op=ALU.subtract),
                     reads=[r_nf, r_scrs[sl]], writes=[r_escrs[sl]])
            else:
                p.op("V", lambda v, vx=vx, cc=cc, sl=sl, rbc=rbc: v.scalar_tensor_tensor(
                    scrs[sl][:], vx, mcol[:, cc:cc + 1], rbc, op0=ALU.subtract, op1=ALU.mult),
                    reads=[c.r_x[cc], r_st], writes=[r_scrs[sl]])
            for dc in range(NDC):
                tp, rtp = c.tp_slot(tp_i)
                si = scrs[sl][:, dc * 128:(dc + 1) * 128]
                so = c.xln[:, dc * S + cc * 128: dc * S + cc * 128 + 128]
                p.op("T", lambda t, si=si, tp=tp: t.transpose(tp, si, c.ident[:]),
                     reads=[r_scrs[sl], c.r_ident], writes=[rtp])
                if tp_i % 2 == 0:
                    p.op("V", lambda v, so=so, tp=tp: v.tensor_copy(so, tp),
                         reads=[rtp], writes=[c.r_xln[dc * NTILE + cc]])
                else:
                    p.op("S", lambda s, so=so, tp=tp: s.copy(so, tp),
                         reads=[rtp], writes=[c.r_xln[dc * NTILE + cc]])
                tp_i += 1
                if need_err:
                    tpe, rtpe = c.tp_slot(tp_i)
                    sie = escrs[sl][:, dc * 128:(dc + 1) * 128]
                    soe = xerr[:, dc * S + cc * 128: dc * S + cc * 128 + 128]
                    p.op("T", lambda t, sie=sie, tpe=tpe: t.transpose(tpe, sie, c.ident[:]),
                         reads=[r_escrs[sl], c.r_ident], writes=[rtpe])
                    if tp_i % 2 == 0:
                        p.op("V", lambda v, soe=soe, tpe=tpe: v.tensor_copy(soe, tpe),
                             reads=[rtpe], writes=[r_xerr[dc * NTILE + cc]])
                    else:
                        p.op("S", lambda s, soe=soe, tpe=tpe: s.copy(soe, tpe),
                             reads=[rtpe], writes=[r_xerr[dc * NTILE + cc]])
                    tp_i += 1
    return xerr, r_xerr


def emit_ln_gate(c: Core, ins):
    k, p = c.k, c.k.p
    c.arena.reset()
    c.farena.reset()
    xsq1, r_xsq1 = c.farena.alloc(1024)
    xsq2, r_xsq2 = c.farena.alloc(1024)
    xsqs, r_xsqs = [xsq1, xsq2], [r_xsq1, r_xsq2]
    nf, r_nf = c.farena.alloc(1024)
    gt, r_gt = c.farena.alloc(128)
    et, r_et = c.farena.alloc(128)
    sc_, r_sc = c.farena.alloc(128)
    bt, r_bt = c.farena.alloc(128)
    sel, r_sel = c.farena.alloc(128)
    gst, r_gst = c.farena.alloc(80)
    c._gate = (gt, r_gt, et, r_et, sc_, r_sc, bt, r_bt, sel, r_sel, gst, r_gst)
    scr, r_scrs = c.arena.alloc(2 * 1024, nres=2)
    scrs = [scr[:, 0:1024], scr[:, 1024:2048]]
    escr, r_escrs = c.arena.alloc(2 * 1024, nres=2)
    escrs = [escr[:, 0:1024], escr[:, 1024:2048]]
    xet, r_xets = c.arena.alloc(2 * NDC * 128, nres=2)
    xets = [xet[:, 0:NDC * 128], xet[:, NDC * 128:]]

    gwhi, r_gwhi = c.smalls["gwhi"], c.r_smalls["gwhi"]
    gwlo, r_gwlo = c.smalls["gwlo"], c.r_smalls["gwlo"]

    st, r_st = c.small("lnstat", [128, 9 * 16], F32)
    sums, ssq, mcol, ex2, var, rstd, msq, vare, lnv = (
        st[:, i * 16:(i + 1) * 16] for i in range(9))
    inv = 1.0 / D

    tp_i = 0
    for g4 in range(NTILE // 4):
        cg = slice(g4 * 4, g4 * 4 + 4)
        for cc in range(g4 * 4, g4 * 4 + 4):
            vx = c.xt[:, cc * D:(cc + 1) * D]
            sl = cc % 2
            p.op("V", lambda v, vx=vx, cc=cc: v.reduce_sum(
                sums[:, cc:cc + 1], vx, axis=AX.X),
                reads=[c.r_x[cc]], writes=[r_st])
            p.op("S", lambda s, vx=vx, sl=sl, cc=cc: s.activation(
                xsqs[sl][:], vx, AF.Square, accum_out=ssq[:, cc:cc + 1]),
                reads=[c.r_x[cc]], writes=[r_xsqs[sl], r_st])
        p.op("V", lambda v, cg=cg: v.tensor_scalar(mcol[:, cg], sums[:, cg], inv, None, op0=ALU.mult),
             reads=[r_st], writes=[r_st])
        p.op("V", lambda v, cg=cg: v.tensor_scalar(ex2[:, cg], ssq[:, cg], inv, None, op0=ALU.mult),
             reads=[r_st], writes=[r_st])
        p.op("V", lambda v, cg=cg: v.tensor_tensor(msq[:, cg], mcol[:, cg], mcol[:, cg], op=ALU.mult),
             reads=[r_st], writes=[r_st])
        p.op("V", lambda v, cg=cg: v.tensor_tensor(var[:, cg], ex2[:, cg], msq[:, cg], op=ALU.subtract),
             reads=[r_st], writes=[r_st])
        p.op("V", lambda v, cg=cg: v.tensor_scalar(vare[:, cg], var[:, cg], 1e-5, None, op0=ALU.add),
             reads=[r_st], writes=[r_st])
        p.op("S", lambda s, cg=cg: s.activation(lnv[:, cg], vare[:, cg], AF.Sqrt),
             reads=[r_st], writes=[r_st])
        p.op("V", lambda v, cg=cg: v.reciprocal(rstd[:, cg], lnv[:, cg]),
             reads=[r_st], writes=[r_st])

        for cc in range(g4 * 4, g4 * 4 + 4):
            vx = c.xt[:, cc * D:(cc + 1) * D]
            sl = cc % 2
            rbc = rstd[:, cc:cc + 1].broadcast_to([128, D])
            p.op("V", lambda v, vx=vx, cc=cc, rbc=rbc: v.scalar_tensor_tensor(
                nf[:], vx, mcol[:, cc:cc + 1], rbc, op0=ALU.subtract, op1=ALU.mult),
                reads=[c.r_x[cc], r_st], writes=[r_nf])
            p.op("V", lambda v, sl=sl: v.tensor_copy(scrs[sl][:], nf[:]),
                 reads=[r_nf], writes=[r_scrs[sl]])
            p.op("V", lambda v, sl=sl: v.tensor_tensor(escrs[sl][:], nf[:], scrs[sl][:], op=ALU.subtract),
                 reads=[r_nf, r_scrs[sl]], writes=[r_escrs[sl]])
            for dc in range(NDC):
                tp, rtp = c.tp_slot(tp_i)
                si = scrs[sl][:, dc * 128:(dc + 1) * 128]
                so = c.xln[:, dc * S + cc * 128: dc * S + cc * 128 + 128]
                p.op("T", lambda t, si=si, tp=tp: t.transpose(tp, si, c.ident[:]),
                     reads=[r_scrs[sl], c.r_ident], writes=[rtp])
                if tp_i % 2 == 0:
                    p.op("V", lambda v, so=so, tp=tp: v.tensor_copy(so, tp),
                         reads=[rtp], writes=[c.r_xln[dc * NTILE + cc]])
                else:
                    p.op("S", lambda s, so=so, tp=tp: s.copy(so, tp),
                         reads=[rtp], writes=[c.r_xln[dc * NTILE + cc]])
                tp_i += 1
                tpe, rtpe = c.tp_slot(tp_i)
                sie = escrs[sl][:, dc * 128:(dc + 1) * 128]
                soe = xets[sl][:, dc * 128:(dc + 1) * 128]
                p.op("T", lambda t, sie=sie, tpe=tpe: t.transpose(tpe, sie, c.ident[:]),
                     reads=[r_escrs[sl], c.r_ident], writes=[rtpe])
                if tp_i % 2 == 0:
                    p.op("V", lambda v, soe=soe, tpe=tpe: v.tensor_copy(soe, tpe),
                         reads=[rtpe], writes=[r_xets[sl]])
                else:
                    p.op("S", lambda s, soe=soe, tpe=tpe: s.copy(soe, tpe),
                         reads=[rtpe], writes=[r_xets[sl]])
                tp_i += 1
            ps = c.PS[2][:, (cc % 2) * 512:(cc % 2) * 512 + 8]
            rps = c.r_PS[2][cc % 2]
            n3 = 3 * NDC
            i = 0
            for dc in range(NDC):
                p.op("T", lambda t, ps=ps, dc=dc, cc=cc, i=i: t.matmul(
                    ps, c.xln[:, dc * S + cc * 128: dc * S + (cc + 1) * 128],
                    gwhi[:, dc * 8:(dc + 1) * 8],
                    start=(i == 0), stop=False),
                    reads=[c.r_xln[dc * NTILE + cc], r_gwhi], writes=[rps])
                i += 1
                p.op("T", lambda t, ps=ps, dc=dc, cc=cc, i=i: t.matmul(
                    ps, c.xln[:, dc * S + cc * 128: dc * S + (cc + 1) * 128],
                    gwlo[:, dc * 8:(dc + 1) * 8],
                    start=False, stop=False),
                    reads=[c.r_xln[dc * NTILE + cc], r_gwlo], writes=[rps])
                i += 1
            for dc in range(NDC):
                p.op("T", lambda t, ps=ps, dc=dc, sl=sl, i=i: t.matmul(
                    ps, xets[sl][:, dc * 128:(dc + 1) * 128],
                    gwhi[:, dc * 8:(dc + 1) * 8],
                    start=False, stop=(i == n3 - 1)),
                    reads=[r_xets[sl], r_gwhi], writes=[rps])
                i += 1
            p.op("V", lambda v, ps=ps, cc=cc: v.tensor_copy(gt[:, cc * 8:(cc + 1) * 8], ps),
                 reads=[rps], writes=[r_gt])



def emit_ffn0(c: Core, wpre_f0):
    k, p = c.k, c.k.p
    c.arena.reset()
    c.farena.reset()
    wr, r_wr = c.arena.alloc(NDC * D)
    wk, r_wk = c.arena.alloc(NDC * D)
    wv, r_wv = c.arena.alloc(NDC * D)
    for (w, r, i) in ((wr, r_wr, 0), (wk, r_wk, 1), (wv, r_wv, 2)):
        p.op("G", lambda g, w=w, i=i: g.dma_start(
            out=w.rearrange("p (dc n) -> p dc n", n=D),
            in_=wpre_f0[i].rearrange("(dc p) n -> p dc n", p=128)),
            writes=[r], dma=True)
    TT = 256
    xr, r_xr = c.arena.alloc(NDC * TT)
    xk, r_xk = c.arena.alloc(NDC * TT)
    ka, r_ka = c.arena.alloc(NDC * TT)
    rt1, r_rt1 = c.arena.alloc(1024)
    rt2, r_rt2 = c.arena.alloc(1024)
    rts, r_rts = [rt1, rt2], [r_rt1, r_rt2]
    kr, r_kr = c.farena.alloc(TT)
    tmp, r_tmp = c.farena.alloc(1024)

    mur, r_mur = c.smalls["mur"], c.r_smalls["mur"]
    omr, r_omr = c.smalls["omr"], c.r_smalls["omr"]
    muk, r_muk = c.smalls["muk"], c.r_smalls["muk"]
    omk, r_omk = c.smalls["omk"], c.r_smalls["omk"]

    n_it = S // TT
    for it in range(n_it):
        T0 = it * TT
        for dc in range(NDC):
            ysl = c.xln[:, dc * S + T0: dc * S + T0 + TT]
            rx = [c.r_xln[dc * NTILE + (T0 // 128) + j] for j in range(TT // 128)]
            rxm1 = c.r_xln[dc * NTILE + (T0 - 1) // 128] if T0 > 0 else None
            for (dst, rdst, mu, rmu, om, rom) in (
                    (xr, r_xr, mur, r_mur, omr, r_omr),
                    (xk, r_xk, muk, r_muk, omk, r_omk)):
                dsl = dst[:, dc * TT: dc * TT + TT]
                p.op("V", lambda v, dsl=dsl, ysl=ysl, mu=mu, dc=dc: v.tensor_scalar(
                    dsl, ysl, mu[:, dc:dc + 1], None, op0=ALU.mult),
                    reads=rx + [rmu], writes=[rdst])
                if it == 0:
                    yprev = c.xln[:, dc * S: dc * S + TT - 1]
                    dsl2 = dst[:, dc * TT + 1: dc * TT + TT]
                    p.op("V", lambda v, dsl2=dsl2, yprev=yprev, om=om, dc=dc: v.scalar_tensor_tensor(
                        dsl2, yprev, om[:, dc:dc + 1], dsl2, op0=ALU.mult, op1=ALU.add),
                        reads=rx + [rom, rdst], writes=[rdst])
                else:
                    yprev = c.xln[:, dc * S + T0 - 1: dc * S + T0 + TT - 1]
                    dsl2 = dst[:, dc * TT: dc * TT + TT]
                    p.op("V", lambda v, dsl2=dsl2, yprev=yprev, om=om, dc=dc: v.scalar_tensor_tensor(
                        dsl2, yprev, om[:, dc:dc + 1], dsl2, op0=ALU.mult, op1=ALU.add),
                        reads=rx + ([rxm1] if rxm1 else []) + [rom, rdst], writes=[rdst])
        for kc in range(NDC):
            sl = kc % 2
            ps = c.PS[0][:, sl * 512: sl * 512 + TT]
            rps = c.r_PS[0][sl]
            for dc in range(NDC):
                p.op("T", lambda t, ps=ps, kc=kc, dc=dc: t.matmul(
                    ps, wk[:, dc * D + kc * 128: dc * D + kc * 128 + 128],
                    xk[:, dc * TT: dc * TT + TT],
                    start=(dc == 0), stop=(dc == NDC - 1)),
                    reads=[r_wk, r_xk], writes=[rps])
            p.op("S", lambda s, ps=ps: s.activation(kr[:], ps, AF.Relu),
                 reads=[rps], writes=[r_kr])
            kasl = ka[:, kc * TT: kc * TT + TT]
            p.op("S", lambda s, kasl=kasl: s.activation(kasl, kr[:], AF.Square),
                 reads=[r_kr], writes=[r_ka])
        for tl in range(TT // 128):
            ts = T0 // 128 + tl
            sl = ts % 2
            psr = c.PS[1]
            rpsr = c.r_PS[1]
            pso = c.PS[2]
            rpso = c.r_PS[2]
            for seg in range(2):
                for dc in range(NDC):
                    p.op("T", lambda t, seg=seg, dc=dc, tl=tl: t.matmul(
                        psr[:, seg * 512:(seg + 1) * 512],
                        xr[:, dc * TT + tl * 128: dc * TT + tl * 128 + 128],
                        wr[:, dc * D + seg * 512: dc * D + (seg + 1) * 512],
                        start=(dc == 0), stop=(dc == NDC - 1)),
                        reads=[r_xr, r_wr], writes=[rpsr[seg]])
            p.op("S", lambda s, sl=sl: s.activation(rts[sl][:, 0:512], psr[:, 0:512], AF.Sigmoid),
                 reads=[rpsr[0]], writes=[r_rts[sl]])
            p.op("S", lambda s, sl=sl: s.activation(rts[sl][:, 512:1024], psr[:, 512:1024], AF.Sigmoid),
                 reads=[rpsr[1]], writes=[r_rts[sl]])
            for seg in range(2):
                for kc in range(NDC):
                    p.op("T", lambda t, seg=seg, kc=kc, tl=tl: t.matmul(
                        pso[:, seg * 512:(seg + 1) * 512],
                        ka[:, kc * TT + tl * 128: kc * TT + tl * 128 + 128],
                        wv[:, kc * D + seg * 512: kc * D + (seg + 1) * 512],
                        start=(kc == 0), stop=(kc == NDC - 1)),
                        reads=[r_ka, r_wv], writes=[rpso[seg]])
            for seg in range(2):
                p.op("V", lambda v, seg=seg, sl=sl: v.tensor_tensor(
                    tmp[:, seg * 512:(seg + 1) * 512], pso[:, seg * 512:(seg + 1) * 512],
                    rts[sl][:, seg * 512:(seg + 1) * 512], op=ALU.mult),
                    reads=[rpso[seg], r_rts[sl]], writes=[r_tmp])
            xv = c.xt[:, ts * D:(ts + 1) * D]
            p.op("V", lambda v, xv=xv: v.tensor_tensor(xv, xv, tmp[:], op=ALU.add),
                 reads=[r_tmp, c.r_x[ts]], writes=[c.r_x[ts]])



def emit_attn(c: Core, lname, wpre_at, ctxd, dnm, vd):
    k, p = c.k, c.k.p
    c.arena.reset()
    c.farena.reset()
    qb8, r_qb8 = c.smalls[lname + "_qb8"], c.r_smalls[lname + "_qb8"]
    kb, r_kb = c.smalls[lname + "_kb"], c.r_smalls[lname + "_kb"]
    bv, r_bv = c.smalls[lname + "_bv"], c.r_smalls[lname + "_bv"]
    bo, r_bo = c.smalls[lname + "_bo"], c.r_smalls[lname + "_bo"]
    cb, r_cb = c.small("cbias", [128, 16], F32)
    cbs, r_cbs = c.small("cbias_s", [1, 64], F32)
    cbb, r_cbb = c.small("cbias_b", [1, 16], BF16)

    wbig, r_wbig = c.arena.alloc(NDC * D)
    wqk, r_wqks = c.arena.alloc(2 * 2 * NDC * 128, nres=2)
    wqkb = [wqk[:, 0:2 * NDC * 128], wqk[:, 2 * NDC * 128:]]
    qc, r_qc = c.arena.alloc(S)
    kc_, r_kc = c.arena.alloc(S)
    va, r_va = c.arena.alloc(NTILE * 130)
    pt1, r_pt1 = c.arena.alloc(1024)
    pt2, r_pt2 = c.arena.alloc(1024)
    pts, r_pts = [pt1, pt2], [r_pt1, r_pt2]
    sq, r_sq = c.arena.alloc(S)
    stg, r_stg = c.arena.alloc(S)
    cin1, r_cin1 = c.arena.alloc(1024)
    cin2, r_cin2 = c.arena.alloc(1024)
    cins, r_cins = [cin1, cin2], [r_cin1, r_cin2]
    cn1, r_cn1 = c.arena.alloc(1024)
    cn2, r_cn2 = c.arena.alloc(1024)
    cns, r_cns = [cn1, cn2], [r_cn1, r_cn2]
    dnr, r_dnr = c.farena.alloc(2048)
    rb1, r_rb1 = c.farena.alloc(1024)
    rb2, r_rb2 = c.farena.alloc(1024)
    rbs, r_rbs = [rb1, rb2], [r_rb1, r_rb2]
    rc, r_rc = c.small("rcrow", [1, S], F32)
    onescol, r_onescol = c.smalls["onescol"], c.r_smalls["onescol"]

    if not hasattr(c, "r_ctxd"):
        c.r_ctxd = [Res() for _ in range(NH)]
        c.r_dnm = [Res() for _ in range(NH)]
        c.r_vd = Res()
    r_ctxd, r_dnm = c.r_ctxd, c.r_dnm

    vones = va.rearrange("p (t u s) -> p t u s", u=2, s=65)[:, :, :, 64:65]
    p.op("V", lambda v: v.memset(vones, 1.0), writes=[r_va])

    WQ, WK, WV, WO = 0, 1, 2, 3
    p.op("G", lambda g: g.dma_start(
        out=wbig.rearrange("p (dc n) -> p dc n", n=D),
        in_=wpre_at[WV].rearrange("(dc p) n -> p dc n", p=128)),
        writes=[r_wbig], dma=True)
    for ts in range(NTILE):
        psn = ts % 3
        psV = c.PS[psn]
        rV = c.r_PS[psn]
        for dc in range(NDC):
            for seg in range(2):
                p.op("T", lambda t, psV=psV, dc=dc, ts=ts, seg=seg: t.matmul(
                    psV[:, seg * 512:(seg + 1) * 512],
                    c.xln[:, dc * S + ts * 128: dc * S + (ts + 1) * 128],
                    wbig[:, dc * D + seg * 512: dc * D + (seg + 1) * 512],
                    start=(dc == 0), stop=False),
                    reads=[r_wbig, c.r_xln[dc * NTILE + ts]], writes=[rV[seg]])
        for seg in range(2):
            p.op("T", lambda t, psV=psV, seg=seg: t.matmul(
                psV[:, seg * 512:(seg + 1) * 512],
                c.ones_row[:], bv[0:1, seg * 512:(seg + 1) * 512],
                start=False, stop=True),
                reads=[c.r_ones, r_bv], writes=[rV[seg]])
        p.op("S", lambda s, psV=psV: s.copy(stg[:, 0:1024], psV[:, 0:1024]),
             reads=[rV[0], rV[1]], writes=[r_stg])
        p.op("SY", lambda s, ts=ts: s.dma_start(out=vd[ts], in_=stg[:, 0:1024]),
             reads=[r_stg], writes=[c.r_vd], dma=True)
    p.op("G", lambda g: g.dma_start(
        out=wbig.rearrange("p (dc n) -> p dc n", n=D),
        in_=wpre_at[WO].rearrange("(dc p) n -> p dc n", p=128)),
        writes=[r_wbig], dma=True)

    for g_ in range(8):
        wb, r_wb = wqkb[g_ % 2], r_wqks[g_ % 2]
        for wi, W in enumerate((WQ, WK)):
            src = wpre_at[W].rearrange("(dc p) n -> p dc n", p=128)[:, :, g_ * 128:(g_ + 1) * 128]
            dst = wb[:, wi * NDC * 128:(wi + 1) * NDC * 128]
            p.op("G", lambda g, dst=dst, src=src: g.dma_start(out=dst, in_=src),
                 writes=[r_wb], dma=True)
        wq_g = wb[:, 0:NDC * 128]
        wk_g = wb[:, NDC * 128:2 * NDC * 128]
        for u in range(2):
            vsrc = vd[:, :, g_ * 128 + u * 64: g_ * 128 + u * 64 + 64].rearrange("t p j -> p t j")
            vadst = va.rearrange("p (t u s) -> p t u s", u=2, s=65)[:, :, u, 0:64]
            p.op("SY", lambda s, vsrc=vsrc, vadst=vadst: s.dma_start(out=vadst, in_=vsrc),
                 reads=[c.r_vd], writes=[r_va], dma=True)

        for which in range(2):
            wsrc = wq_g if which == 0 else wk_g
            dst, r_dst = (qc, r_qc) if which == 0 else (kc_, r_kc)
            bias, r_bias = (qb8, r_qb8) if which == 0 else (kb, r_kb)
            for dc in range(NDC):
                for seg in range(4):
                    ps = c.PS[seg // 2][:, (seg % 2) * 512:(seg % 2) * 512 + 512]
                    rps = c.r_PS[seg // 2][seg % 2]
                    p.op("T", lambda t, ps=ps, wsrc=wsrc, dc=dc, seg=seg: t.matmul(
                        ps, wsrc[:, dc * 128:(dc + 1) * 128],
                        c.xln[:, dc * S + seg * 512: dc * S + (seg + 1) * 512],
                        start=(dc == 0), stop=(dc == NDC - 1)),
                        reads=[r_wb] + [c.r_xln[dc * NTILE + seg * 4 + j] for j in range(4)],
                        writes=[rps])
            for seg in range(4):
                ps = c.PS[seg // 2][:, (seg % 2) * 512:(seg % 2) * 512 + 512]
                rps = c.r_PS[seg // 2][seg % 2]
                p.op("S", lambda s, ps=ps, dst=dst, seg=seg, bias=bias, g_=g_: s.activation(
                    dst[:, seg * 512:(seg + 1) * 512], ps, AF.Identity,
                    bias=bias[:, g_:g_ + 1], scale=0.3535533905932738),
                    reads=[rps, r_bias], writes=[r_dst])

        for which in range(2):
            src, r_src = (qc, r_qc) if which == 0 else (kc_, r_kc)
            p.op("S", lambda s, src=src: s.activation(sq[:], src[:], AF.Square),
                 reads=[r_src], writes=[r_sq])
            for hp in range(2):
                base = hp * 64
                col0 = which * 2 + hp
                for rnd in range(4):
                    ps = c.PS[2][0:1, (rnd % 2) * 512:(rnd % 2) * 512 + 512]
                    rps = c.r_PS[2][rnd % 2]
                    p.op("T", lambda t, ps=ps, base=base, rnd=rnd: t.matmul(
                        ps, onescol[base:base + 64, 0:1],
                        sq[base:base + 64, rnd * 512:(rnd + 1) * 512],
                        start=True, stop=True),
                        reads=[r_sq, r_onescol], writes=[rps])
                    p.op("V", lambda v, ps=ps, col0=col0, rnd=rnd: v.reduce_max(
                        cbs[0:1, col0 * 8 + rnd: col0 * 8 + rnd + 1], ps, axis=AX.X),
                        reads=[rps], writes=[r_cbs])
                p.op("V", lambda v, col0=col0: v.tensor_tensor(
                    cbs[0:1, col0 * 8 + 4: col0 * 8 + 5],
                    cbs[0:1, col0 * 8 + 0: col0 * 8 + 1],
                    cbs[0:1, col0 * 8 + 1: col0 * 8 + 2], op=ALU.max),
                    reads=[r_cbs], writes=[r_cbs])
                p.op("V", lambda v, col0=col0: v.tensor_tensor(
                    cbs[0:1, col0 * 8 + 5: col0 * 8 + 6],
                    cbs[0:1, col0 * 8 + 2: col0 * 8 + 3],
                    cbs[0:1, col0 * 8 + 3: col0 * 8 + 4], op=ALU.max),
                    reads=[r_cbs], writes=[r_cbs])
                p.op("V", lambda v, col0=col0: v.tensor_tensor(
                    cbs[0:1, col0 * 8 + 2: col0 * 8 + 3],
                    cbs[0:1, col0 * 8 + 4: col0 * 8 + 5],
                    cbs[0:1, col0 * 8 + 5: col0 * 8 + 6], op=ALU.max),
                    reads=[r_cbs], writes=[r_cbs])
        for hp in range(2):
            h = 2 * g_ + hp
            qcol, kcol = (0 * 2 + hp) * 8 + 2, (1 * 2 + hp) * 8 + 2
            p.op("V", lambda v, qcol=qcol, kcol=kcol, hp=hp: v.tensor_tensor(
                cbs[0:1, 32 + hp: 33 + hp], cbs[0:1, qcol:qcol + 1],
                cbs[0:1, kcol:kcol + 1], op=ALU.add),
                reads=[r_cbs], writes=[r_cbs])
            p.op("V", lambda v, hp=hp, h=h: v.tensor_scalar(
                cbb[0:1, h:h + 1], cbs[0:1, 32 + hp:33 + hp], -0.5, None, op0=ALU.mult),
                reads=[r_cbs], writes=[r_cbb])
            ps = c.PS[2][:, 256 + hp:257 + hp]
            rps = c.r_PS[2][0]
            p.op("T", lambda t, ps=ps, h=h: t.matmul(
                ps, c.ones_row[:], cbb[0:1, h:h + 1], start=True, stop=True),
                reads=[c.r_ones, r_cbb], writes=[rps])
            p.op("V", lambda v, ps=ps, h=h: v.tensor_copy(cb[:, h:h + 1], ps),
                 reads=[rps], writes=[r_cb])

        for hp in range(2):
            h = 2 * g_ + hp
            base = hp * 64
            for qh in range(2):
                def s_ops(kt, base=base, qh=qh, h=h):
                    psS = c.PS[kt % 2]
                    rS = c.r_PS[kt % 2]
                    for seg in range(2):
                        p.op("T", lambda t, kt=kt, seg=seg, psS=psS, base=base, qh=qh: t.matmul(
                            psS[:, seg * 512:(seg + 1) * 512],
                            kc_[base:base + 64, kt * 128:(kt + 1) * 128],
                            qc[base:base + 64, qh * 1024 + seg * 512: qh * 1024 + (seg + 1) * 512],
                            start=True, stop=True),
                            reads=[r_kc, r_qc], writes=[rS[seg]])
                    p.op("S", lambda s, kt=kt, psS=psS, h=h: s.activation(
                        pts[kt % 2][:], psS[:, :], AF.Exp, bias=cb[:, h:h + 1]),
                        reads=[rS[0], rS[1], r_cb], writes=[r_pts[kt % 2]])

                def c_ops(kt, hp=hp):
                    for seg in range(2):
                        p.op("T", lambda t, kt=kt, seg=seg, hp=hp: t.matmul(
                            c.PS[2][0:65, seg * 512:(seg + 1) * 512],
                            va[:, kt * 130 + hp * 65: kt * 130 + hp * 65 + 65],
                            pts[kt % 2][:, seg * 512:(seg + 1) * 512],
                            start=(kt == 0), stop=(kt == NTILE - 1)),
                            reads=[r_va, r_pts[kt % 2]], writes=[c.r_PS[2][seg]])

                s_ops(0)
                for kt in range(NTILE):
                    if kt + 1 < NTILE:
                        s_ops(kt + 1)
                    c_ops(kt)
                p.op("V", lambda v, qh=qh: v.tensor_copy(
                    stg[0:64, qh * 1024:(qh + 1) * 1024], c.PS[2][0:64, :]),
                    reads=[c.r_PS[2][0], c.r_PS[2][1]], writes=[r_stg])
                p.op("V", lambda v, qh=qh: v.tensor_copy(
                    dnr[0:1, qh * 1024:(qh + 1) * 1024], c.PS[2][64:65, :]),
                    reads=[c.r_PS[2][0], c.r_PS[2][1]], writes=[r_dnr])
            p.op("V", lambda v: v.reciprocal(rc[0:1, :], dnr[0:1, 0:2048]),
                 reads=[r_dnr], writes=[r_rc])
            p.op("SY", lambda s, h=h: s.dma_start(out=dnm[h][None, :], in_=rc[0:1, :]),
                 reads=[r_rc], writes=[r_dnm[h]], dma=True)
            p.op("SY", lambda s, h=h: s.dma_start(out=ctxd[h], in_=stg[0:64, :]),
                 reads=[r_stg], writes=[r_ctxd[h]], dma=True)

    ctxv = ctxd.rearrange("(dc hp) r (nt j) -> hp r nt dc j", hp=2, nt=NTILE)
    dnv = dnm.rearrange("(dc hp) (nt j) -> hp nt dc j", hp=2, nt=NTILE)
    for ts in range(NTILE):
        sl = ts % 2
        for hp in range(2):
            cdst = cins[sl][64 * hp:64 * (hp + 1), :].rearrange("r (dc j) -> r dc j", j=128)
            p.op("SY", lambda s, ts=ts, sl=sl, hp=hp, cdst=cdst: s.dma_start(
                out=cdst, in_=ctxv[hp, :, ts]),
                reads=[r_ctxd[2 * dc + hp] for dc in range(NDC)],
                writes=[r_cins[sl]], dma=True)
            rdst = rbs[sl][64 * hp:64 * (hp + 1), :].rearrange("r (dc j) -> r dc j", j=128)
            p.op("SY", lambda s, ts=ts, sl=sl, hp=hp, rdst=rdst: s.dma_start(
                out=rdst, in_=dnv[hp, ts][None, :, :].broadcast_to([64, NDC, 128])),
                reads=[r_dnm[2 * dc + hp] for dc in range(NDC)],
                writes=[r_rbs[sl]], dma=True)
        p.op("V", lambda v, sl=sl: v.tensor_tensor(cns[sl][:], cins[sl][:], rbs[sl][:], op=ALU.mult),
             reads=[r_cins[sl], r_rbs[sl]], writes=[r_cns[sl]])
        psO = c.PS[sl]
        rO = c.r_PS[sl]
        for dc in range(NDC):
            for seg in range(2):
                p.op("T", lambda t, sl=sl, seg=seg, dc=dc, psO=psO: t.matmul(
                    psO[:, seg * 512:(seg + 1) * 512],
                    cns[sl][:, dc * 128:(dc + 1) * 128],
                    wbig[:, dc * D + seg * 512: dc * D + (seg + 1) * 512],
                    start=(dc == 0), stop=False),
                    reads=[r_cns[sl], r_wbig], writes=[rO[seg]])
        for seg in range(2):
            p.op("T", lambda t, sl=sl, seg=seg, psO=psO: t.matmul(
                psO[:, seg * 512:(seg + 1) * 512],
                c.ones_row[:], bo[0:1, seg * 512:(seg + 1) * 512],
                start=False, stop=True),
                reads=[c.r_ones, r_bo], writes=[rO[seg]])
        xv = c.xt[:, ts * D:(ts + 1) * D]
        p.op("V", lambda v, xv=xv, psO=psO: v.tensor_tensor(xv, xv, psO[:, :], op=ALU.add),
             reads=[rO[0], rO[1], c.r_x[ts]], writes=[c.r_x[ts]])



MOE_THIRDS = ((0, 4), (4, 8), (8, 11))


def emit_moe(c: Core, ins):
    k, p = c.k, c.k.p
    gbb, r_gbb = c.smalls["gbb"], c.r_smalls["gbb"]
    b1c, r_b1c = c.smalls["b1c"], c.r_smalls["b1c"]
    b3c, r_b3c = c.smalls["b3c"], c.r_smalls["b3c"]
    b2a, r_b2a = c.smalls["b2a"], c.r_smalls["b2a"]
    sb2r, r_sb2r = c.smalls["sb2r"], c.r_smalls["sb2r"]

    (gt, r_gt, et, r_et, sc, r_sc, bt, r_bt, sel, r_sel, gst, r_gst) = c._gate
    msk, r_msk = et, r_et
    rw, r_rw = c.small("moe_rw", [128, 128], F32)
    rwb, r_rwb = c.small("moe_rwb", [128, 128], BF16)
    nm, ssum, rs, m1, m2 = (gst[:, i * 16:(i + 1) * 16] for i in range(5))
    gt3 = gt.rearrange("p (s e) -> p s e", e=8)
    p.op("V", lambda v: v.reduce_max(nm, gt3, axis=AX.X, negate=True),
         reads=[r_gt], writes=[r_gst])
    for ts in range(NTILE):
        p.op("S", lambda s, ts=ts: s.activation(
            et[:, ts * 8:(ts + 1) * 8], gt[:, ts * 8:(ts + 1) * 8], AF.Exp,
            bias=nm[:, ts:ts + 1]),
            reads=[r_gt, r_gst], writes=[r_et])
    et3 = et.rearrange("p (s e) -> p s e", e=8)
    p.op("V", lambda v: v.reduce_sum(ssum, et3, axis=AX.X),
         reads=[r_et], writes=[r_gst])
    p.op("V", lambda v: v.reciprocal(rs, ssum), reads=[r_gst], writes=[r_gst])
    for ts in range(NTILE):
        p.op("V", lambda v, ts=ts: v.tensor_scalar(
            sc[:, ts * 8:(ts + 1) * 8], et[:, ts * 8:(ts + 1) * 8],
            rs[:, ts:ts + 1], None, op0=ALU.mult),
            reads=[r_et, r_gst], writes=[r_sc])
    sc3 = sc.rearrange("p (s e) -> p s e", e=8)
    bt3 = bt.rearrange("p (s e) -> p s e", e=8)
    p.op("V", lambda v: v.tensor_tensor(
        bt3, sc3, gbb[:, None, :].broadcast_to([128, NTILE, 8]), op=ALU.add),
        reads=[r_sc, r_gbb], writes=[r_bt])
    p.op("V", lambda v: v.reduce_max(m1, bt3, axis=AX.X), reads=[r_bt], writes=[r_gst])
    for ts in range(NTILE):
        p.op("V", lambda v, ts=ts: v.tensor_scalar(
            sel[:, ts * 8:(ts + 1) * 8], bt[:, ts * 8:(ts + 1) * 8],
            m1[:, ts:ts + 1], None, op0=ALU.is_ge),
            reads=[r_bt, r_gst], writes=[r_sel])
    p.op("V", lambda v: v.scalar_tensor_tensor(msk, sel, -1e30, bt, op0=ALU.mult, op1=ALU.add),
         reads=[r_sel, r_bt], writes=[r_msk])
    msk3 = msk.rearrange("p (s e) -> p s e", e=8)
    p.op("V", lambda v: v.reduce_max(m2, msk3, axis=AX.X), reads=[r_msk], writes=[r_gst])
    for ts in range(NTILE):
        p.op("V", lambda v, ts=ts: v.tensor_scalar(
            sel[:, ts * 8:(ts + 1) * 8], bt[:, ts * 8:(ts + 1) * 8],
            m2[:, ts:ts + 1], None, op0=ALU.is_ge),
            reads=[r_bt, r_gst], writes=[r_sel])
    p.op("V", lambda v: v.scalar_tensor_tensor(rw, sel, 0.5, sc, op0=ALU.mult, op1=ALU.mult),
         reads=[r_sel, r_sc], writes=[r_rw])
    p.op("V", lambda v: v.tensor_copy(rwb, rw), reads=[r_rw], writes=[r_rwb])

    c.arena.reset()
    c.farena.reset()
    w1s, r_w1s = c.arena.alloc(2 * NDC * 512, nres=2)
    w1b = [w1s[:, 0:NDC * 512], w1s[:, NDC * 512:]]
    w3s, r_w3s = c.arena.alloc(2 * NDC * 512, nres=2)
    w3b = [w3s[:, 0:NDC * 512], w3s[:, NDC * 512:]]
    w2s, r_w2s0 = c.arena.alloc(4 * D)
    w2b = [w2s, w2s]
    r_w2s = [r_w2s0, r_w2s0]
    aa, r_aa = c.arena.alloc(4 * S)
    rwT, r_rwT = c.arena.alloc(S)
    p1s, r_p1s = c.farena.alloc(2048)
    p3s, r_p3s = c.farena.alloc(2048)
    for ts in range(NTILE):
        tp, rtp = c.tp_slot(ts)
        p.op("T", lambda t, tp=tp, ts=ts: t.transpose(
            tp[0:8, 0:128], rwb[:, ts * 8:(ts + 1) * 8], c.ident[:]),
            reads=[r_rwb, c.r_ident], writes=[rtp])
        p.op("V", lambda v, tp=tp, ts=ts: v.tensor_copy(
            rwT[0:8, ts * 128:(ts + 1) * 128], tp[0:8, 0:128]),
            reads=[rtp], writes=[r_rwT])

    third_i = 0
    for e in range(9):
        shared = (e == 8)
        w1_ap = ins["L1_ffn_sw1"] if shared else ins["L1_ffn_w1"][e]
        w3_ap = ins["L1_ffn_sw3"] if shared else ins["L1_ffn_w3"][e]
        w2_ap = ins["L1_ffn_sw2"] if shared else ins["L1_ffn_w2"][e]
        for (h0, h1) in MOE_THIRDS:
            hlen = h1 - h0
            hcols = hlen * 128
            sl = third_i % 2
            third_i += 1
            p.op("G", lambda g, sl=sl, w1_ap=w1_ap, h0=h0, h1=h1, hcols=hcols: g.dma_start(
                out=w1b[sl][:, 0:NDC * hcols].rearrange("p (dc h) -> p dc h", h=hcols),
                in_=w1_ap[:, h0 * 128:h1 * 128].rearrange("(dc p) h -> p dc h", p=128)),
                writes=[r_w1s[sl]], dma=True)
            p.op("G", lambda g, sl=sl, w3_ap=w3_ap, h0=h0, h1=h1, hcols=hcols: g.dma_start(
                out=w3b[sl][:, 0:NDC * hcols].rearrange("p (dc h) -> p dc h", h=hcols),
                in_=w3_ap[:, h0 * 128:h1 * 128].rearrange("(dc p) h -> p dc h", p=128)),
                writes=[r_w3s[sl]], dma=True)
            p.op("G", lambda g, sl=sl, w2_ap=w2_ap, h0=h0, h1=h1: g.dma_start(
                out=w2b[sl][:, 0:(h1 - h0) * D].rearrange("p (hc d) -> p hc d", d=D),
                in_=w2_ap[h0 * 128:h1 * 128, :].rearrange("(hc p) d -> p hc d", p=128)),
                writes=[r_w2s[sl]], dma=True)
            for hcl in range(hlen):
                bi = 8 * 11 + (h0 + hcl) if shared else e * 11 + (h0 + hcl)
                for dc in range(NDC):
                    for th in range(2):
                        for seg in range(2):
                            ps = c.PS[th][:, seg * 512:(seg + 1) * 512]
                            rps = c.r_PS[th][seg]
                            p.op("T", lambda t, ps=ps, sl=sl, dc=dc, hcl=hcl,
                                 th=th, seg=seg, hcols=hcols: t.matmul(
                                     ps, w1b[sl][:, dc * hcols + hcl * 128: dc * hcols + (hcl + 1) * 128],
                                     c.xln[:, dc * S + th * 1024 + seg * 512: dc * S + th * 1024 + (seg + 1) * 512],
                                     start=(dc == 0), stop=(dc == NDC - 1)),
                                reads=[r_w1s[sl]] + [c.r_xln[dc * NTILE + th * 8 + seg * 4 + j] for j in range(4)],
                                writes=[rps])
                for th in range(2):
                    p.op("S", lambda s, bi=bi, th=th: s.activation(
                        p1s[:, th * 1024:(th + 1) * 1024], c.PS[th][:, :],
                        AF.Identity, bias=b1c[:, bi:bi + 1]),
                        reads=[c.r_PS[th][0], c.r_PS[th][1], r_b1c], writes=[r_p1s])
                for th in range(2):
                    for dc in range(NDC):
                        for seg in range(2):
                            ps = c.PS[2][:, seg * 512:(seg + 1) * 512]
                            rps = c.r_PS[2][seg]
                            p.op("T", lambda t, ps=ps, sl=sl, dc=dc, hcl=hcl,
                                 th=th, seg=seg, hcols=hcols: t.matmul(
                                     ps, w3b[sl][:, dc * hcols + hcl * 128: dc * hcols + (hcl + 1) * 128],
                                     c.xln[:, dc * S + th * 1024 + seg * 512: dc * S + th * 1024 + (seg + 1) * 512],
                                     start=(dc == 0), stop=(dc == NDC - 1)),
                                reads=[r_w3s[sl]] + [c.r_xln[dc * NTILE + th * 8 + seg * 4 + j] for j in range(4)],
                                writes=[rps])
                    p.op("V", lambda v, bi=bi, th=th: v.tensor_scalar(
                        p3s[:, th * 1024:(th + 1) * 1024], c.PS[2][:, :],
                        b3c[:, bi:bi + 1], None, op0=ALU.add),
                        reads=[c.r_PS[2][0], c.r_PS[2][1], r_b3c], writes=[r_p3s])
                p.op("V", lambda v: v.tensor_tensor(p1s[:], p1s[:], p3s[:], op=ALU.mult),
                     reads=[r_p1s, r_p3s], writes=[r_p1s])
                adst = aa[:, hcl * S: (hcl + 1) * S]
                p.op("S", lambda s, adst=adst: s.activation(adst, p1s[:], AF.Silu),
                     reads=[r_p1s], writes=[r_aa])
            for ts in range(NTILE):
                psn = ts % 3
                psD = c.PS[psn]
                rD = c.r_PS[psn]
                first = True
                for seg in range(2):
                    if shared and h0 == 0:
                        p.op("T", lambda t, psD=psD, seg=seg: t.matmul(
                            psD[:, seg * 512:(seg + 1) * 512],
                            c.ones_row[:], sb2r[0:1, seg * 512:(seg + 1) * 512],
                            start=True, stop=False),
                            reads=[c.r_ones, r_sb2r], writes=[rD[seg]])
                    for hcl in range(hlen):
                        p.op("T", lambda t, psD=psD, sl=sl, seg=seg, hcl=hcl, ts=ts,
                             h0=h0, shared=shared, hlen=hlen: t.matmul(
                            psD[:, seg * 512:(seg + 1) * 512],
                            aa[:, hcl * S + ts * 128:hcl * S + (ts + 1) * 128],
                            w2b[sl][:, hcl * D + seg * 512: hcl * D + (seg + 1) * 512],
                            start=(hcl == 0 and not (shared and h0 == 0)),
                            stop=(hcl == hlen - 1)),
                            reads=[r_aa, r_w2s[sl]], writes=[rD[seg]])
                xv = c.xt[:, ts * D:(ts + 1) * D]
                scal = 0.5 if shared else None
                if shared:
                    p.op("V", lambda v, xv=xv, psD=psD: v.scalar_tensor_tensor(
                        xv, psD[:, :], 0.5, xv, op0=ALU.mult, op1=ALU.add),
                        reads=[rD[0], rD[1], c.r_x[ts]], writes=[c.r_x[ts]])
                else:
                    p.op("V", lambda v, xv=xv, psD=psD, ts=ts, e=e: v.scalar_tensor_tensor(
                        xv, psD[:, :], rw[:, ts * 8 + e: ts * 8 + e + 1], xv,
                        op0=ALU.mult, op1=ALU.add),
                        reads=[rD[0], rD[1], r_rw, c.r_x[ts]], writes=[c.r_x[ts]])
    for ts in range(NTILE):
        psn = ts % 3
        psD = c.PS[psn]
        rD = c.r_PS[psn]
        for seg in range(2):
            p.op("T", lambda t, psD=psD, seg=seg, ts=ts: t.matmul(
                psD[:, seg * 512:(seg + 1) * 512],
                rwT[0:8, ts * 128:(ts + 1) * 128],
                b2a[0:8, seg * 512:(seg + 1) * 512],
                start=True, stop=True),
                reads=[r_rwT, r_b2a], writes=[rD[seg]])
        xv = c.xt[:, ts * D:(ts + 1) * D]
        p.op("V", lambda v, xv=xv, psD=psD: v.tensor_tensor(xv, xv, psD[:, :], op=ALU.add),
             reads=[rD[0], rD[1], c.r_x[ts]], writes=[c.r_x[ts]])



def emit_consts(c: Core):
    k, p = c.k, c.k.p
    import ml_dtypes
    ident_d = k.const("c_ident_bf",
                      np.eye(128, dtype=np.float32).astype(ml_dtypes.bfloat16))
    ones_d = k.const("c_ones_bf",
                     np.ones((1, 128), dtype=np.float32).astype(ml_dtypes.bfloat16))
    onesc_d = k.const("c_onesc_bf",
                      np.ones((128, 1), dtype=np.float32).astype(ml_dtypes.bfloat16))
    p.op("G", lambda g: g.dma_start(out=c.ident[:], in_=ident_d),
         writes=[c.r_ident], dma=True)
    p.op("G", lambda g: g.dma_start(out=c.ones_row[:], in_=ones_d),
         writes=[c.r_ones], dma=True)
    oc, r_oc = c.small("onescol", [128, 1], BF16)
    p.op("G", lambda g: g.dma_start(out=oc[:], in_=onesc_d),
         writes=[r_oc], dma=True)


def emit_prep_ffn0(c: Core, ins):
    p = c.k.p
    mur, r_mur = emit_col_prep(c, ins["L0_ffn_mu_r"], D, "mur")
    muk, r_muk = emit_col_prep(c, ins["L0_ffn_mu_k"], D, "muk")
    for (nm, mu, rmu) in (("omr", mur, r_mur), ("omk", muk, r_muk)):
        om, r_om = c.small(nm, [128, NDC], F32)
        p.op("V", lambda v, om=om, mu=mu: v.tensor_scalar(
            om[:], mu[:], 1.0, -1.0, op0=ALU.subtract, op1=ALU.mult),
            reads=[rmu], writes=[r_om])


def emit_prep_attn(c: Core, ins, lname):
    if lname == "l0":
        qb, kb, bv, bo = (ins["L0_attn_q_b"], ins["L0_attn_k_b"],
                          ins["L0_attn_v_b"], ins["L0_attn_out_b"])
    else:
        qb, kb, bv, bo = (ins["L1_attn_bq"], ins["L1_attn_bk"],
                          ins["L1_attn_bv"], ins["L1_attn_bo"])
    emit_col_prep(c, qb, D, lname + "_qb8", scale=0.3535533905932738)
    emit_col_prep(c, kb, D, lname + "_kb", scale=0.3535533905932738)
    emit_row_prep(c, bv, D, lname + "_bv")
    emit_row_prep(c, bo, D, lname + "_bo")


def emit_prep_moe(c: Core, ins):
    k, p = c.k, c.k.p
    gstg, r_gstg = c.prep_scratch("gwstg", D)
    gwhi, r_gwhi = c.small("gwhi", [128, NDC * 8], BF16)
    gwlo, r_gwlo = c.small("gwlo", [128, NDC * 8], BF16)
    identf, r_idf = c.prep_scratch("identf", 128)
    if r_idf.lw is None:
        idd = k.const("c_ident_f32", np.eye(128, dtype=np.float32))
        p.op("G", lambda g: g.dma_start(out=identf[:], in_=idd),
             writes=[r_idf], dma=True)
    p.op("G", lambda g: g.dma_start(out=gstg[0:8, :], in_=ins["L1_ffn_gate_w"]),
         writes=[r_gstg], dma=True)
    for dc in range(NDC):
        ps = c.PS[2][:, 0:128]
        rps = c.r_PS[2][0]
        p.op("T", lambda t, ps=ps, dc=dc: t.transpose(
            ps, gstg[:, dc * 128:(dc + 1) * 128], identf[:]),
            reads=[r_gstg, r_idf], writes=[rps])
        p.op("V", lambda v, ps=ps, dc=dc: v.tensor_copy(gwhi[:, dc * 8:(dc + 1) * 8], ps[:, 0:8]),
             reads=[rps], writes=[r_gwhi])
        p.op("V", lambda v, ps=ps, dc=dc: v.tensor_tensor(
            gwlo[:, dc * 8:(dc + 1) * 8], ps[:, 0:8], gwhi[:, dc * 8:(dc + 1) * 8],
            op=ALU.subtract),
            reads=[rps, r_gwhi], writes=[r_gwlo])
    gbb, r_gbb = c.small("gbb", [128, 8], F32)
    p.op("G", lambda g: g.dma_start(
        out=gbb[:], in_=ins["L1_ffn_gate_b"][None, :].broadcast_to([128, 8])),
        writes=[r_gbb], dma=True)
    b1c, r_b1c = c.small("b1c", [128, 9 * NHC], F32)
    b3c, r_b3c = c.small("b3c", [128, 9 * NHC], F32)
    for (dst, rdst, arr2, shared1) in ((b1c, r_b1c, "L1_ffn_b1", "L1_ffn_sb1"),
                                       (b3c, r_b3c, "L1_ffn_b3", "L1_ffn_sb3")):
        for e in range(9):
            src = ins[shared1] if e == 8 else ins[arr2][e]
            stage, r_stage = c.prep_scratch("colstage", 128)
            p.op("G", lambda g, src=src: g.dma_start(
                out=stage[0:NHC, :], in_=src.rearrange("(cc p) -> cc p", p=128)),
                writes=[r_stage], dma=True)
            ps = c.PS[2][:, 0:128]
            rps = c.r_PS[2][0]
            p.op("T", lambda t, ps=ps: t.transpose(ps, stage[:], identf[:]),
                 reads=[r_stage, r_idf], writes=[rps])
            p.op("V", lambda v, ps=ps, dst=dst, e=e: v.tensor_copy(
                dst[:, e * NHC:(e + 1) * NHC], ps[:, 0:NHC]),
                reads=[rps], writes=[rdst])
    b2a, r_b2a = c.small("b2a", [8, D], BF16)
    p.op("G", lambda g: g.dma_start(out=b2a[:], in_=ins["L1_ffn_b2"]),
         writes=[r_b2a], dma=True)
    emit_row_prep(c, ins["L1_ffn_sb2"], D, "sb2r")



def declare_inputs(k: K, names_shapes):
    ins = {}
    for nm, shape in names_shapes.items():
        ins[nm] = k.dram(nm, list(shape), F32, kind="ExternalInput")
    return ins


IN_SHAPES = {
    "L0_attn_q_w": (D, D), "L0_attn_q_b": (D,), "L0_attn_k_w": (D, D),
    "L0_attn_k_b": (D,), "L0_attn_v_w": (D, D), "L0_attn_v_b": (D,),
    "L0_attn_out_w": (D, D), "L0_attn_out_b": (D,),
    "L0_ffn_W_r": (D, D), "L0_ffn_W_k": (D, D), "L0_ffn_W_v": (D, D),
    "L0_ffn_mu_r": (D,), "L0_ffn_mu_k": (D,),
    "L1_attn_Wq": (D, D), "L1_attn_bq": (D,), "L1_attn_Wk": (D, D),
    "L1_attn_bk": (D,), "L1_attn_Wv": (D, D), "L1_attn_bv": (D,),
    "L1_attn_Wo": (D, D), "L1_attn_bo": (D,),
    "L1_ffn_gate_w": (NE, D), "L1_ffn_gate_b": (NE,),
    "L1_ffn_w1": (NE, D, HM), "L1_ffn_b1": (NE, HM),
    "L1_ffn_w3": (NE, D, HM), "L1_ffn_b3": (NE, HM),
    "L1_ffn_w2": (NE, HM, D), "L1_ffn_b2": (NE, D),
    "L1_ffn_sw1": (D, HM), "L1_ffn_sb1": (HM,), "L1_ffn_sw3": (D, HM),
    "L1_ffn_sb3": (HM,), "L1_ffn_sw2": (HM, D), "L1_ffn_sb2": (D,),
}

SUB_INPUTS = {
    "ffn0": ["L0_ffn_W_r", "L0_ffn_W_k", "L0_ffn_W_v", "L0_ffn_mu_r", "L0_ffn_mu_k"],
    "attn0": ["L0_attn_q_w", "L0_attn_q_b", "L0_attn_k_w", "L0_attn_k_b",
              "L0_attn_v_w", "L0_attn_v_b", "L0_attn_out_w", "L0_attn_out_b"],
    "attn1": ["L1_attn_Wq", "L1_attn_bq", "L1_attn_Wk", "L1_attn_bk",
              "L1_attn_Wv", "L1_attn_bv", "L1_attn_Wo", "L1_attn_bo"],
    "moe": ["L1_ffn_gate_w", "L1_ffn_gate_b", "L1_ffn_w1", "L1_ffn_b1",
            "L1_ffn_w3", "L1_ffn_b3", "L1_ffn_w2", "L1_ffn_b2",
            "L1_ffn_sw1", "L1_ffn_sb1", "L1_ffn_sw3", "L1_ffn_sb3",
            "L1_ffn_sw2", "L1_ffn_sb2"],
}


def build_sublayer_program(kind):
    k = K()
    x_in = k.dram("x", [B_PER_CORE, S, D], F32, kind="ExternalInput")
    y_out = k.dram("y", [B_PER_CORE, S, D], F32, kind="ExternalOutput")
    ins = declare_inputs(k, {nm: IN_SHAPES[nm] for nm in SUB_INPUTS[kind]})
    c = Core(k)
    emit_consts(c)

    if kind == "ffn0":
        emit_prep_ffn0(c, ins)
        wpre = k.dram("wpre_f0", [3, D, D], BF16)
        for i, nm in enumerate(("L0_ffn_W_r", "L0_ffn_W_k", "L0_ffn_W_v")):
            emit_prepass_matrix(c, ins[nm], wpre[i], scale=(0.5 if i == 2 else None))
        for b in range(B_PER_CORE):
            emit_load_x(c, x_in, b)
            emit_ln(c)
            emit_ffn0(c, wpre)
            emit_store_x(c, y_out, b)
    elif kind in ("attn0", "attn1"):
        lname = "l0" if kind == "attn0" else "l1"
        emit_prep_attn(c, ins, lname)
        wpre = k.dram("wpre_at", [4, D, D], BF16)
        wnames = (("L0_attn_q_w", "L0_attn_k_w", "L0_attn_v_w", "L0_attn_out_w")
                  if kind == "attn0" else
                  ("L1_attn_Wq", "L1_attn_Wk", "L1_attn_Wv", "L1_attn_Wo"))
        for i, nm in enumerate(wnames):
            emit_prepass_matrix(c, ins[nm], wpre[i])
        ctxd = k.dram("ctxd", [NH, 64, S], BF16)
        dnm = k.dram("dnm", [NH, S], F32)
        vd = k.dram("vd", [NTILE, 128, D], BF16)
        for b in range(B_PER_CORE):
            emit_load_x(c, x_in, b)
            emit_ln(c)
            emit_attn(c, lname, wpre, ctxd, dnm, vd)
            emit_store_x(c, y_out, b)
    elif kind == "moe":
        emit_prep_moe(c, ins)
        for b in range(B_PER_CORE):
            emit_load_x(c, x_in, b)
            emit_ln_gate(c, ins)
            emit_moe(c, ins)
            emit_store_x(c, y_out, b)
    k.p.emit()
    return k.nc


_PROG_CACHE = {}


def _get_prog(kind):
    if kind not in _PROG_CACHE:
        _PROG_CACHE[kind] = build_sublayer_program(kind)
    return _PROG_CACHE[kind]


def kernel(**inputs):
    x = np.ascontiguousarray(inputs["L0_attn_query"], dtype=np.float32)
    shards = np.split(x, NCORE, axis=0)
    seqs = ["ffn0", "attn0", "ffn0", "moe", "attn1", "moe"]
    cur = shards
    for kind in seqs:
        nc = _get_prog(kind)
        wmap = {nm: np.ascontiguousarray(inputs[nm], dtype=np.float32)
                for nm in SUB_INPUTS[kind]}
        in_maps = [dict(wmap, x=cur[i]) for i in range(NCORE)]
        res = run_bass_kernel_spmd(nc, in_maps, core_ids=list(range(NCORE)))
        cur = [np.asarray(r["y"]) for r in res.results]
    return np.concatenate(cur, axis=0)



def build_fused_program():
    k = K()
    x_in = k.dram("x", [B_PER_CORE, S, D], F32, kind="ExternalInput")
    y_out = k.dram("y", [B_PER_CORE, S, D], F32, kind="ExternalOutput")
    ins = declare_inputs(k, IN_SHAPES)
    c = Core(k)
    emit_consts(c)
    emit_prep_ffn0(c, ins)
    emit_prep_attn(c, ins, "l0")
    emit_prep_attn(c, ins, "l1")
    emit_prep_moe(c, ins)

    wpre_f0 = k.dram("wpre_f0", [3, D, D], BF16)
    wpre_a0 = k.dram("wpre_a0", [4, D, D], BF16)
    wpre_a1 = k.dram("wpre_a1", [4, D, D], BF16)
    for i, nm in enumerate(("L0_ffn_W_r", "L0_ffn_W_k", "L0_ffn_W_v")):
        emit_prepass_matrix(c, ins[nm], wpre_f0[i], scale=(0.5 if i == 2 else None))
    for i, nm in enumerate(("L0_attn_q_w", "L0_attn_k_w", "L0_attn_v_w", "L0_attn_out_w")):
        emit_prepass_matrix(c, ins[nm], wpre_a0[i])
    for i, nm in enumerate(("L1_attn_Wq", "L1_attn_Wk", "L1_attn_Wv", "L1_attn_Wo")):
        emit_prepass_matrix(c, ins[nm], wpre_a1[i])

    ctxd = k.dram("ctxd", [NH, 64, S], BF16)
    dnm = k.dram("dnm", [NH, S], F32)
    vd = k.dram("vd", [NTILE, 128, D], BF16)

    for b in range(B_PER_CORE):
        emit_load_x(c, x_in, b)
        emit_ln(c)
        emit_ffn0(c, wpre_f0)
        emit_ln(c)
        emit_attn(c, "l0", wpre_a0, ctxd, dnm, vd)
        emit_ln(c)
        emit_ffn0(c, wpre_f0)
        emit_ln_gate(c, ins)
        emit_moe(c, ins)
        emit_ln(c)
        emit_attn(c, "l1", wpre_a1, ctxd, dnm, vd)
        emit_ln_gate(c, ins)
        emit_moe(c, ins)
        emit_store_x(c, y_out, b)
    k.p.emit()
    return k.nc


_FUSED_CACHE = []


def kernel(**inputs):
    x = np.ascontiguousarray(inputs["L0_attn_query"], dtype=np.float32)
    shards = np.split(x, NCORE, axis=0)
    if not _FUSED_CACHE:
        _FUSED_CACHE.append(build_fused_program())
    nc = _FUSED_CACHE[0]
    wmap = {nm: np.ascontiguousarray(inputs[nm], dtype=np.float32)
            for nm in IN_SHAPES}
    in_maps = [dict(wmap, x=shards[i]) for i in range(NCORE)]
    res = run_bass_kernel_spmd(nc, in_maps, core_ids=list(range(NCORE)))
    return np.concatenate([np.asarray(r["y"]) for r in res.results], axis=0)


# version 09
# speedup vs baseline: 1.1807x; 1.1807x over previous
import math
import numpy as np

import concourse.bass as bass
import concourse.mybir as mybir
from concourse.bass_utils import run_bass_kernel_spmd

F32 = mybir.dt.float32
BF16 = mybir.dt.bfloat16
AF = mybir.ActivationFunctionType
ALU = mybir.AluOpType
AX = mybir.AxisListType

NCORE = 8
B_PER_CORE = 2
S = 2048
D = 1024
NH = 16
DH = 64
HM = 1408
NE = 8
NTILE = S // 128
NDC = D // 128
NHC = HM // 128

ENGS = ("T", "V", "S", "SY", "G")



class Op:
    __slots__ = ("eng", "fn", "deps", "dma", "need_inc", "val", "waits", "clock", "i")

    def __init__(self, eng, fn, deps, dma, i):
        self.eng = eng
        self.fn = fn
        self.deps = deps
        self.dma = dma
        self.need_inc = False
        self.val = None
        self.waits = None
        self.clock = None
        self.i = i


class Res:
    __slots__ = ("lw", "readers")

    def __init__(self):
        self.lw = None
        self.readers = []


class Prog:
    def __init__(self, nc):
        self.nc = nc
        self.ops = []

    def op(self, eng, fn, reads=(), writes=(), dma=False):
        deps = []
        for r in reads:
            if r.lw is not None:
                deps.append(r.lw)
        for w in writes:
            deps.extend(w.readers)
            if w.lw is not None:
                deps.append(w.lw)
        o = Op(eng, fn, deps, dma, len(self.ops))
        self.ops.append(o)
        for r in reads:
            r.readers.append(o)
        for w in writes:
            w.lw = o
            w.readers = []
        return o

    @staticmethod
    def _free_dep(d, o):
        return d.eng == o.eng and not d.dma and not o.dma and o.eng == "T"

    def schedule(self):
        for o in self.ops:
            for d in o.deps:
                if not self._free_dep(d, o):
                    d.need_inc = True
        POOL = 8
        counts = {}
        for o in self.ops:
            if o.dma:
                j = counts.get((o.eng, "dj"), 0)
                counts[(o.eng, "dj")] = j + 1
                key = (o.eng, "d%d" % (j % POOL))
                c = counts.get(key, 0) + 16
                counts[key] = c
                o.val = (key, c)
            elif o.need_inc:
                key = (o.eng, "c")
                c = counts.get(key, 0) + 1
                counts[key] = c
                o.val = (key, c)
        marks = {e: {} for e in ENGS}
        for o in self.ops:
            wm = marks[o.eng]
            need = {}
            for d in o.deps:
                if self._free_dep(d, o):
                    continue
                k, v = d.val
                if wm.get(k, 0) < v and need.get(k, 0) < v:
                    need[k] = v
            for d in o.deps:
                if self._free_dep(d, o):
                    continue
                if d.clock:
                    for k, v in d.clock.items():
                        if wm.get(k, 0) < v:
                            wm[k] = v
            for k, v in need.items():
                if wm.get(k, 0) >= v:
                    continue
                wm[k] = v
            o.waits = sorted((k, v) for k, v in need.items() if v > 0)
            o.clock = dict(wm)
        self.counts = counts
        return counts

    def emit(self):
        nc = self.nc
        counts = self.schedule()

        sem_keys = sorted({o.val[0] for o in self.ops if o.val is not None}
                          | {k for o in self.ops for (k, _) in (o.waits or [])})
        sems = {}
        import contextlib
        stack = contextlib.ExitStack()
        for k in sem_keys:
            sems[k] = stack.enter_context(nc.semaphore(f"sem_{k[0]}_{k[1]}"))

        by_eng = {e: [o for o in self.ops if o.eng == e] for e in ENGS}

        def emit_for(eng_obj, eng_name):
            for o in by_eng[eng_name]:
                for k, v in o.waits:
                    eng_obj.wait_ge(sems[k], v)
                ins = o.fn(eng_obj)
                if o.dma:
                    ins.then_inc(sems[o.val[0]], 16)
                elif o.need_inc:
                    ins.then_inc(sems[o.val[0]], 1)
            for s in range(8):
                dk = (eng_name, "d%d" % s)
                if dk in counts and counts[dk] > 0 and dk in sems:
                    eng_obj.wait_ge(sems[dk], counts[dk])

        with nc.Block() as block:
            @block.tensor
            def _(t):
                emit_for(t, "T")

            @block.vector
            def _(v):
                emit_for(v, "V")

            @block.scalar
            def _(s):
                emit_for(s, "S")

            @block.sync
            def _(s):
                emit_for(s, "SY")

            @block.gpsimd
            def _(g):
                emit_for(g, "G")
        stack.close()

    def verify(self):
        idx = {e: 0 for e in ENGS}
        done = [False] * len(self.ops)
        sems = {}
        by_eng = {e: [o for o in self.ops if o.eng == e] for e in ENGS}
        pending = []
        bad = []
        while True:
            progressed = False
            order = sorted(ENGS, key=lambda e: 0 if (
                idx[e] < len(by_eng[e]) and
                any(not done[d.i] for d in by_eng[e][idx[e]].deps)) else 1)
            for e in order:
                if idx[e] >= len(by_eng[e]):
                    continue
                o = by_eng[e][idx[e]]
                if all(sems.get(k, 0) >= v for k, v in (o.waits or [])):
                    for d in o.deps:
                        if not done[d.i]:
                            bad.append((o.i, d.i, o.eng, d.eng))
                    if o.dma:
                        pending.append(o)
                    else:
                        done[o.i] = True
                        if o.val is not None:
                            k, v = o.val
                            sems[k] = sems.get(k, 0) + 1
                    idx[e] += 1
                    progressed = True
                    break
            if not progressed:
                if pending:
                    o = pending.pop(0)
                    done[o.i] = True
                    k, v = o.val
                    sems[k] = sems.get(k, 0) + 16
                else:
                    break
        stuck = [e for e in ENGS if idx[e] < len(by_eng[e])]
        return bad, stuck

    def barrier_guard(self, reslist):
        last = {}
        for r in reslist:
            cand = list(r.readers) + ([r.lw] if r.lw else [])
            for o in cand:
                k = (o.eng, o.dma)
                if k not in last or last[k].i < o.i:
                    last[k] = o
        return list(last.values())


class Arena:

    def __init__(self, prog, ap, cols):
        self.prog = prog
        self.ap = ap
        self.cols = cols
        self.off = 0
        self.live = []
        self.guard = []

    def alloc(self, ncols, nres=1):
        assert self.off + ncols <= self.cols, \
            f"arena overflow: {self.off}+{ncols} > {self.cols}"
        view = self.ap[:, self.off:self.off + ncols]
        self.off += ncols
        rs = [Res() for _ in range(nres)]
        for r in rs:
            r.readers = list(self.guard)
        self.live.extend(rs)
        return (view, rs[0]) if nres == 1 else (view, rs)

    def reset(self):
        g = self.prog.barrier_guard(self.live)
        self.guard = g
        self.live = []
        self.off = 0



def cdiv(a, b):
    return (a + b - 1) // b


class K:

    def __init__(self, name="mk"):
        self.nc = bass.Bass("TRN2", target_bir_lowering=False)
        self.p = Prog(self.nc)
        self._alloc = []
        self.consts = {}

    def sb(self, name, shape, dtype):
        h = self.nc.alloc_sbuf_tensor(name, shape, dtype)
        return h.ap()

    def psum(self, name, shape, dtype):
        h = self.nc.alloc_psum_tensor(name, shape, dtype)
        return h.ap()

    def dram(self, name, shape, dtype, kind="Internal"):
        return self.nc.dram_tensor(name, shape, dtype, kind=kind).ap()

    def const(self, name, arr):
        if name not in self.consts:
            self.consts[name] = self.nc.inline_tensor(arr, name=name).ap()
        return self.consts[name]


def rn(*parts):
    return [Res() for _ in range(parts[0])] if len(parts) == 1 else None



class Core:

    def __init__(self, k: K, n_seq=B_PER_CORE):
        self.k = k
        nc, p = k.nc, k.p
        self.n_seq = n_seq

        self.xt = k.sb("xt", [128, NTILE * D], F32)
        self.xln = k.sb("xln", [128, NDC * S], BF16)
        self.ab = k.sb("ab", [128, 32768], BF16)
        self.af = k.sb("af", [128, 4096], F32)
        self.r_x = [Res() for _ in range(NTILE)]
        self.r_xln = [Res() for _ in range(NDC * NTILE)]
        self.arena = Arena(p, self.ab, 32768)
        self.farena = Arena(p, self.af, 4096)

        self.PS = [k.psum(f"ps{i}", [128, 1024], F32) for i in range(3)]
        self.r_PS = [[Res(), Res()] for _ in range(3)]
        self.TP = [k.psum(f"tp{i}", [128, 512], BF16) for i in range(2)]
        self.r_TP = [Res(), Res()]

        self.ident = k.sb("identt", [128, 128], BF16)
        self.r_ident = Res()
        self.ones_row = k.sb("ones_row", [1, 128], BF16)
        self.r_ones = Res()
        self.smalls = {}
        self.r_smalls = {}

    def tp_slot(self, i):
        t = i % 2
        s = (i // 2) % 4
        return self.TP[t][:, s * 128:(s + 1) * 128], self.r_TP[t]

    def small(self, name, shape, dtype):
        if name not in self.smalls:
            self.smalls[name] = self.k.sb("sm_" + name, shape, dtype)
            self.r_smalls[name] = Res()
        return self.smalls[name], self.r_smalls[name]

    def prep_scratch(self, name, ncols):
        if not hasattr(self, "_prep"):
            self._prep = {}
        if name not in self._prep:
            self._prep[name] = self.farena.alloc(ncols)
        return self._prep[name]


def emit_consts(c: Core):
    k, p = c.k, c.k.p
    import ml_dtypes
    ident_d = k.const("c_ident_bf",
                      np.eye(128, dtype=np.float32).astype(ml_dtypes.bfloat16))
    ones_d = k.const("c_ones_bf", np.ones((1, 128), dtype=np.float32).astype(ml_dtypes.bfloat16))
    p.op("G", lambda g: g.dma_start(out=c.ident[:], in_=ident_d),
         writes=[c.r_ident], dma=True)
    p.op("G", lambda g: g.dma_start(out=c.ones_row[:], in_=ones_d),
         writes=[c.r_ones], dma=True)


def emit_load_x(c: Core, x_in_ap, b):
    p = c.k.p
    src = x_in_ap[b].rearrange("(c p) d -> p c d", p=128)
    dst = c.xt.rearrange("p (c d) -> p c d", d=D)
    for g4 in range(NTILE // 4):
        sl4 = slice(g4 * 4, g4 * 4 + 4)
        p.op("SY", lambda s, sl4=sl4: s.dma_start(out=dst[:, sl4], in_=src[:, sl4]),
             writes=c.r_x[g4 * 4:g4 * 4 + 4], dma=True)


def emit_store_x(c: Core, x_out_ap, b):
    p = c.k.p
    dst = x_out_ap[b].rearrange("(c p) d -> p c d", p=128)
    src = c.xt.rearrange("p (c d) -> p c d", d=D)
    for g4 in range(NTILE // 4):
        sl4 = slice(g4 * 4, g4 * 4 + 4)
        p.op("SY", lambda s, sl4=sl4: s.dma_start(out=dst[:, sl4], in_=src[:, sl4]),
             reads=c.r_x[g4 * 4:g4 * 4 + 4], dma=True)


def emit_col_prep(c: Core, src_ap, n, name, scale=None, psslot=0):
    k, p = c.k, c.k.p
    m = n // 128
    stage, r_stage = c.prep_scratch("colstage", 128)
    col, r_col = c.small(name, [128, m], F32)
    p.op("G", lambda g: g.dma_start(out=stage[0:m, :], in_=src_ap.rearrange("(c p) -> c p", p=128)),
         writes=[r_stage], dma=True)
    ps = c.PS[psslot]
    rps = c.r_PS[psslot][0]
    identf, r_idf = c.prep_scratch("identf", 128)
    if r_idf.lw is None:
        idd = k.const("c_ident_f32", np.eye(128, dtype=np.float32))
        p.op("G", lambda g: g.dma_start(out=identf[:], in_=idd),
             writes=[r_idf], dma=True)
    p.op("T", lambda t: t.transpose(ps[:, 0:128], stage[:], identf[:]),
         reads=[r_stage, r_idf], writes=[rps])
    if scale is None:
        p.op("V", lambda v: v.tensor_copy(col[:], ps[:, 0:m]),
             reads=[rps], writes=[r_col])
    else:
        p.op("V", lambda v: v.tensor_scalar(col[:], ps[:, 0:m], float(scale), None, op0=ALU.mult),
             reads=[rps], writes=[r_col])
    return col, r_col


def emit_row_prep(c: Core, src_ap, n, name, scale=None):
    k, p = c.k, c.k.p
    rf0, r_rf = c.prep_scratch("rowstage", D)
    rf = rf0[0:1, 0:n]
    rb, r_rb = c.small(name, [1, n], BF16)
    p.op("G", lambda g: g.dma_start(out=rf, in_=src_ap[None, :]),
         writes=[r_rf], dma=True)
    sc = 1.0 if scale is None else float(scale)
    p.op("V", lambda v: v.tensor_scalar(rb[:], rf, float(sc), None, op0=ALU.mult),
         reads=[r_rf], writes=[r_rb])
    return rb, r_rb


def emit_prepass_matrix(c: Core, src_ap, dst_dram, scale=None, tpbase=0):
    k, p = c.k, c.k.p
    a, b = src_ap.shape
    ac, bc = a // 128, b // 128
    if not hasattr(c, "_prepbufs"):
        w0, r0 = c.arena.alloc(4 * NDC * D, nres=4)
        c._prepbufs = [(w0[:, i * NDC * D:(i + 1) * NDC * D], r0[i]) for i in range(4)]
        c._prep_i = 0
    wsrc, r_src = c._prepbufs[2 * (c._prep_i % 2)]
    wdst, r_dst = c._prepbufs[2 * (c._prep_i % 2) + 1]
    c._prep_i += 1
    p.op("G", lambda g: g.dma_start(
        out=wsrc.rearrange("p (rc n) -> p rc n", n=b), 
        in_=src_ap.rearrange("(rc p) n -> p rc n", p=128)),
        writes=[r_src], dma=True)
    cnt = 0
    for i in range(ac):
        for j in range(bc):
            tp, rtp = c.tp_slot(tpbase + cnt)
            cnt += 1
            sl_in = wsrc[:, i * b + j * 128: i * b + j * 128 + 128]
            sl_out = wdst[:, j * a + i * 128: j * a + i * 128 + 128]
            p.op("T", lambda t, si=sl_in, tpx=tp: t.transpose(tpx, si, c.ident[:]),
                 reads=[r_src, c.r_ident], writes=[rtp])
            eng = "V" if (cnt % 2 == 0) else "S"
            if scale is None:
                if eng == "V":
                    p.op("V", lambda v, so=sl_out, tpx=tp: v.tensor_copy(so, tpx),
                         reads=[rtp], writes=[r_dst])
                else:
                    p.op("S", lambda s, so=sl_out, tpx=tp: s.copy(so, tpx),
                         reads=[rtp], writes=[r_dst])
            else:
                p.op("V", lambda v, so=sl_out, tpx=tp: v.tensor_scalar(so, tpx, float(scale), None, op0=ALU.mult),
                     reads=[rtp], writes=[r_dst])
    p.op("SY", lambda s: s.dma_start(
        out=dst_dram.rearrange("(bch p) m -> p bch m", p=128),
        in_=wdst.rearrange("p (bch m) -> p bch m", m=a)),
        reads=[r_dst], dma=True)
    return cnt



def emit_ln(c: Core, need_err=False, tpbase=0):
    k, p = c.k, c.k.p
    c.arena.reset()
    c.farena.reset()
    xsq1, r_xsq1 = c.farena.alloc(1024)
    xsq2, r_xsq2 = c.farena.alloc(1024)
    xsqs, r_xsqs = [xsq1, xsq2], [r_xsq1, r_xsq2]
    nf, r_nf = c.farena.alloc(1024)
    scr, r_scrs = c.arena.alloc(2 * 1024, nres=2)
    scrs = [scr[:, 0:1024], scr[:, 1024:2048]]
    if need_err:
        escr, r_escrs = c.arena.alloc(2 * 1024, nres=2)
        escrs = [escr[:, 0:1024], escr[:, 1024:2048]]
        xerr, r_xerr = c.arena.alloc(NDC * S, nres=NDC * NTILE)
    else:
        xerr, r_xerr = None, None

    st, r_st = c.small("lnstat", [128, 9 * 16], F32)
    sums, ssq, mcol, ex2, var, rstd, msq, vare, lnv = (
        st[:, i * 16:(i + 1) * 16] for i in range(9))
    inv = 1.0 / D

    tp_i = tpbase
    for g4 in range(NTILE // 4):
        cg = slice(g4 * 4, g4 * 4 + 4)
        for cc in range(g4 * 4, g4 * 4 + 4):
            vx = c.xt[:, cc * D:(cc + 1) * D]
            sl = cc % 2
            p.op("V", lambda v, vx=vx, cc=cc: v.reduce_sum(
                sums[:, cc:cc + 1], vx, axis=AX.X),
                reads=[c.r_x[cc]], writes=[r_st])
            p.op("S", lambda s, vx=vx, sl=sl, cc=cc: s.activation(
                xsqs[sl][:], vx, AF.Square, accum_out=ssq[:, cc:cc + 1]),
                reads=[c.r_x[cc]], writes=[r_xsqs[sl], r_st])
        p.op("V", lambda v, cg=cg: v.tensor_scalar(mcol[:, cg], sums[:, cg], inv, None, op0=ALU.mult),
             reads=[r_st], writes=[r_st])
        p.op("V", lambda v, cg=cg: v.tensor_scalar(ex2[:, cg], ssq[:, cg], inv, None, op0=ALU.mult),
             reads=[r_st], writes=[r_st])
        p.op("V", lambda v, cg=cg: v.tensor_tensor(msq[:, cg], mcol[:, cg], mcol[:, cg], op=ALU.mult),
             reads=[r_st], writes=[r_st])
        p.op("V", lambda v, cg=cg: v.tensor_tensor(var[:, cg], ex2[:, cg], msq[:, cg], op=ALU.subtract),
             reads=[r_st], writes=[r_st])
        p.op("V", lambda v, cg=cg: v.tensor_scalar(vare[:, cg], var[:, cg], 1e-5, None, op0=ALU.add),
             reads=[r_st], writes=[r_st])
        p.op("S", lambda s, cg=cg: s.activation(lnv[:, cg], vare[:, cg], AF.Sqrt),
             reads=[r_st], writes=[r_st])
        p.op("V", lambda v, cg=cg: v.reciprocal(rstd[:, cg], lnv[:, cg]),
             reads=[r_st], writes=[r_st])

        for cc in range(g4 * 4, g4 * 4 + 4):
            vx = c.xt[:, cc * D:(cc + 1) * D]
            sl = cc % 2
            rbc = rstd[:, cc:cc + 1].broadcast_to([128, D])
            if need_err:
                p.op("V", lambda v, vx=vx, cc=cc, rbc=rbc: v.scalar_tensor_tensor(
                    nf[:], vx, mcol[:, cc:cc + 1], rbc, op0=ALU.subtract, op1=ALU.mult),
                    reads=[c.r_x[cc], r_st], writes=[r_nf])
                p.op("V", lambda v, sl=sl: v.tensor_copy(scrs[sl][:], nf[:]),
                     reads=[r_nf], writes=[r_scrs[sl]])
                p.op("V", lambda v, sl=sl: v.tensor_tensor(escrs[sl][:], nf[:], scrs[sl][:], op=ALU.subtract),
                     reads=[r_nf, r_scrs[sl]], writes=[r_escrs[sl]])
            else:
                p.op("V", lambda v, vx=vx, cc=cc, sl=sl, rbc=rbc: v.scalar_tensor_tensor(
                    scrs[sl][:], vx, mcol[:, cc:cc + 1], rbc, op0=ALU.subtract, op1=ALU.mult),
                    reads=[c.r_x[cc], r_st], writes=[r_scrs[sl]])
            for dc in range(NDC):
                tp, rtp = c.tp_slot(tp_i)
                si = scrs[sl][:, dc * 128:(dc + 1) * 128]
                so = c.xln[:, dc * S + cc * 128: dc * S + cc * 128 + 128]
                p.op("T", lambda t, si=si, tp=tp: t.transpose(tp, si, c.ident[:]),
                     reads=[r_scrs[sl], c.r_ident], writes=[rtp])
                if tp_i % 2 == 0:
                    p.op("V", lambda v, so=so, tp=tp: v.tensor_copy(so, tp),
                         reads=[rtp], writes=[c.r_xln[dc * NTILE + cc]])
                else:
                    p.op("S", lambda s, so=so, tp=tp: s.copy(so, tp),
                         reads=[rtp], writes=[c.r_xln[dc * NTILE + cc]])
                tp_i += 1
                if need_err:
                    tpe, rtpe = c.tp_slot(tp_i)
                    sie = escrs[sl][:, dc * 128:(dc + 1) * 128]
                    soe = xerr[:, dc * S + cc * 128: dc * S + cc * 128 + 128]
                    p.op("T", lambda t, sie=sie, tpe=tpe: t.transpose(tpe, sie, c.ident[:]),
                         reads=[r_escrs[sl], c.r_ident], writes=[rtpe])
                    if tp_i % 2 == 0:
                        p.op("V", lambda v, soe=soe, tpe=tpe: v.tensor_copy(soe, tpe),
                             reads=[rtpe], writes=[r_xerr[dc * NTILE + cc]])
                    else:
                        p.op("S", lambda s, soe=soe, tpe=tpe: s.copy(soe, tpe),
                             reads=[rtpe], writes=[r_xerr[dc * NTILE + cc]])
                    tp_i += 1
    return xerr, r_xerr


def emit_ln_gate(c: Core, ins):
    k, p = c.k, c.k.p
    c.arena.reset()
    c.farena.reset()
    xsq1, r_xsq1 = c.farena.alloc(1024)
    xsq2, r_xsq2 = c.farena.alloc(1024)
    xsqs, r_xsqs = [xsq1, xsq2], [r_xsq1, r_xsq2]
    nf, r_nf = c.farena.alloc(1024)
    gt, r_gt = c.farena.alloc(128)
    et, r_et = c.farena.alloc(128)
    sc_, r_sc = c.farena.alloc(128)
    bt, r_bt = c.farena.alloc(128)
    sel, r_sel = c.farena.alloc(128)
    gst, r_gst = c.farena.alloc(80)
    c._gate = (gt, r_gt, et, r_et, sc_, r_sc, bt, r_bt, sel, r_sel, gst, r_gst)
    scr, r_scrs = c.arena.alloc(2 * 1024, nres=2)
    scrs = [scr[:, 0:1024], scr[:, 1024:2048]]
    escr, r_escrs = c.arena.alloc(2 * 1024, nres=2)
    escrs = [escr[:, 0:1024], escr[:, 1024:2048]]
    xet, r_xets = c.arena.alloc(2 * NDC * 128, nres=2)
    xets = [xet[:, 0:NDC * 128], xet[:, NDC * 128:]]

    gwhi, r_gwhi = c.smalls["gwhi"], c.r_smalls["gwhi"]
    gwlo, r_gwlo = c.smalls["gwlo"], c.r_smalls["gwlo"]

    st, r_st = c.small("lnstat", [128, 9 * 16], F32)
    sums, ssq, mcol, ex2, var, rstd, msq, vare, lnv = (
        st[:, i * 16:(i + 1) * 16] for i in range(9))
    inv = 1.0 / D

    tp_i = 0
    for g4 in range(NTILE // 4):
        cg = slice(g4 * 4, g4 * 4 + 4)
        for cc in range(g4 * 4, g4 * 4 + 4):
            vx = c.xt[:, cc * D:(cc + 1) * D]
            sl = cc % 2
            p.op("V", lambda v, vx=vx, cc=cc: v.reduce_sum(
                sums[:, cc:cc + 1], vx, axis=AX.X),
                reads=[c.r_x[cc]], writes=[r_st])
            p.op("S", lambda s, vx=vx, sl=sl, cc=cc: s.activation(
                xsqs[sl][:], vx, AF.Square, accum_out=ssq[:, cc:cc + 1]),
                reads=[c.r_x[cc]], writes=[r_xsqs[sl], r_st])
        p.op("V", lambda v, cg=cg: v.tensor_scalar(mcol[:, cg], sums[:, cg], inv, None, op0=ALU.mult),
             reads=[r_st], writes=[r_st])
        p.op("V", lambda v, cg=cg: v.tensor_scalar(ex2[:, cg], ssq[:, cg], inv, None, op0=ALU.mult),
             reads=[r_st], writes=[r_st])
        p.op("V", lambda v, cg=cg: v.tensor_tensor(msq[:, cg], mcol[:, cg], mcol[:, cg], op=ALU.mult),
             reads=[r_st], writes=[r_st])
        p.op("V", lambda v, cg=cg: v.tensor_tensor(var[:, cg], ex2[:, cg], msq[:, cg], op=ALU.subtract),
             reads=[r_st], writes=[r_st])
        p.op("V", lambda v, cg=cg: v.tensor_scalar(vare[:, cg], var[:, cg], 1e-5, None, op0=ALU.add),
             reads=[r_st], writes=[r_st])
        p.op("S", lambda s, cg=cg: s.activation(lnv[:, cg], vare[:, cg], AF.Sqrt),
             reads=[r_st], writes=[r_st])
        p.op("V", lambda v, cg=cg: v.reciprocal(rstd[:, cg], lnv[:, cg]),
             reads=[r_st], writes=[r_st])

        for cc in range(g4 * 4, g4 * 4 + 4):
            vx = c.xt[:, cc * D:(cc + 1) * D]
            sl = cc % 2
            rbc = rstd[:, cc:cc + 1].broadcast_to([128, D])
            p.op("V", lambda v, vx=vx, cc=cc, rbc=rbc: v.scalar_tensor_tensor(
                nf[:], vx, mcol[:, cc:cc + 1], rbc, op0=ALU.subtract, op1=ALU.mult),
                reads=[c.r_x[cc], r_st], writes=[r_nf])
            p.op("V", lambda v, sl=sl: v.tensor_copy(scrs[sl][:], nf[:]),
                 reads=[r_nf], writes=[r_scrs[sl]])
            p.op("V", lambda v, sl=sl: v.tensor_tensor(escrs[sl][:], nf[:], scrs[sl][:], op=ALU.subtract),
                 reads=[r_nf, r_scrs[sl]], writes=[r_escrs[sl]])
            for dc in range(NDC):
                tp, rtp = c.tp_slot(tp_i)
                si = scrs[sl][:, dc * 128:(dc + 1) * 128]
                so = c.xln[:, dc * S + cc * 128: dc * S + cc * 128 + 128]
                p.op("T", lambda t, si=si, tp=tp: t.transpose(tp, si, c.ident[:]),
                     reads=[r_scrs[sl], c.r_ident], writes=[rtp])
                if tp_i % 2 == 0:
                    p.op("V", lambda v, so=so, tp=tp: v.tensor_copy(so, tp),
                         reads=[rtp], writes=[c.r_xln[dc * NTILE + cc]])
                else:
                    p.op("S", lambda s, so=so, tp=tp: s.copy(so, tp),
                         reads=[rtp], writes=[c.r_xln[dc * NTILE + cc]])
                tp_i += 1
                tpe, rtpe = c.tp_slot(tp_i)
                sie = escrs[sl][:, dc * 128:(dc + 1) * 128]
                soe = xets[sl][:, dc * 128:(dc + 1) * 128]
                p.op("T", lambda t, sie=sie, tpe=tpe: t.transpose(tpe, sie, c.ident[:]),
                     reads=[r_escrs[sl], c.r_ident], writes=[rtpe])
                if tp_i % 2 == 0:
                    p.op("V", lambda v, soe=soe, tpe=tpe: v.tensor_copy(soe, tpe),
                         reads=[rtpe], writes=[r_xets[sl]])
                else:
                    p.op("S", lambda s, soe=soe, tpe=tpe: s.copy(soe, tpe),
                         reads=[rtpe], writes=[r_xets[sl]])
                tp_i += 1
            ps = c.PS[2][:, (cc % 2) * 512:(cc % 2) * 512 + 8]
            rps = c.r_PS[2][cc % 2]
            n3 = 3 * NDC
            i = 0
            for dc in range(NDC):
                p.op("T", lambda t, ps=ps, dc=dc, cc=cc, i=i: t.matmul(
                    ps, c.xln[:, dc * S + cc * 128: dc * S + (cc + 1) * 128],
                    gwhi[:, dc * 8:(dc + 1) * 8],
                    start=(i == 0), stop=False),
                    reads=[c.r_xln[dc * NTILE + cc], r_gwhi], writes=[rps])
                i += 1
                p.op("T", lambda t, ps=ps, dc=dc, cc=cc, i=i: t.matmul(
                    ps, c.xln[:, dc * S + cc * 128: dc * S + (cc + 1) * 128],
                    gwlo[:, dc * 8:(dc + 1) * 8],
                    start=False, stop=False),
                    reads=[c.r_xln[dc * NTILE + cc], r_gwlo], writes=[rps])
                i += 1
            for dc in range(NDC):
                p.op("T", lambda t, ps=ps, dc=dc, sl=sl, i=i: t.matmul(
                    ps, xets[sl][:, dc * 128:(dc + 1) * 128],
                    gwhi[:, dc * 8:(dc + 1) * 8],
                    start=False, stop=(i == n3 - 1)),
                    reads=[r_xets[sl], r_gwhi], writes=[rps])
                i += 1
            p.op("V", lambda v, ps=ps, cc=cc: v.tensor_copy(gt[:, cc * 8:(cc + 1) * 8], ps),
                 reads=[rps], writes=[r_gt])



def emit_ffn0(c: Core, wpre_f0):
    k, p = c.k, c.k.p
    c.arena.reset()
    c.farena.reset()
    wr, r_wr = c.arena.alloc(NDC * D)
    wk, r_wk = c.arena.alloc(NDC * D)
    wv, r_wv = c.arena.alloc(NDC * D)
    for (w, r, i) in ((wr, r_wr, 0), (wk, r_wk, 1), (wv, r_wv, 2)):
        p.op("G", lambda g, w=w, i=i: g.dma_start(
            out=w.rearrange("p (dc n) -> p dc n", n=D),
            in_=wpre_f0[i].rearrange("(dc p) n -> p dc n", p=128)),
            writes=[r], dma=True)
    TT = 256
    xr, r_xr = c.arena.alloc(NDC * TT)
    xk, r_xk = c.arena.alloc(NDC * TT)
    ka, r_ka = c.arena.alloc(NDC * TT)
    rt1, r_rt1 = c.arena.alloc(1024)
    rt2, r_rt2 = c.arena.alloc(1024)
    rts, r_rts = [rt1, rt2], [r_rt1, r_rt2]
    kr, r_kr = c.farena.alloc(TT)
    tmp, r_tmp = c.farena.alloc(1024)

    mur, r_mur = c.smalls["mur"], c.r_smalls["mur"]
    omr, r_omr = c.smalls["omr"], c.r_smalls["omr"]
    muk, r_muk = c.smalls["muk"], c.r_smalls["muk"]
    omk, r_omk = c.smalls["omk"], c.r_smalls["omk"]

    n_it = S // TT
    for it in range(n_it):
        T0 = it * TT
        for dc in range(NDC):
            ysl = c.xln[:, dc * S + T0: dc * S + T0 + TT]
            rx = [c.r_xln[dc * NTILE + (T0 // 128) + j] for j in range(TT // 128)]
            rxm1 = c.r_xln[dc * NTILE + (T0 - 1) // 128] if T0 > 0 else None
            for (dst, rdst, mu, rmu, om, rom) in (
                    (xr, r_xr, mur, r_mur, omr, r_omr),
                    (xk, r_xk, muk, r_muk, omk, r_omk)):
                dsl = dst[:, dc * TT: dc * TT + TT]
                p.op("V", lambda v, dsl=dsl, ysl=ysl, mu=mu, dc=dc: v.tensor_scalar(
                    dsl, ysl, mu[:, dc:dc + 1], None, op0=ALU.mult),
                    reads=rx + [rmu], writes=[rdst])
                if it == 0:
                    yprev = c.xln[:, dc * S: dc * S + TT - 1]
                    dsl2 = dst[:, dc * TT + 1: dc * TT + TT]
                    p.op("V", lambda v, dsl2=dsl2, yprev=yprev, om=om, dc=dc: v.scalar_tensor_tensor(
                        dsl2, yprev, om[:, dc:dc + 1], dsl2, op0=ALU.mult, op1=ALU.add),
                        reads=rx + [rom, rdst], writes=[rdst])
                else:
                    yprev = c.xln[:, dc * S + T0 - 1: dc * S + T0 + TT - 1]
                    dsl2 = dst[:, dc * TT: dc * TT + TT]
                    p.op("V", lambda v, dsl2=dsl2, yprev=yprev, om=om, dc=dc: v.scalar_tensor_tensor(
                        dsl2, yprev, om[:, dc:dc + 1], dsl2, op0=ALU.mult, op1=ALU.add),
                        reads=rx + ([rxm1] if rxm1 else []) + [rom, rdst], writes=[rdst])
        for kc in range(NDC):
            sl = kc % 2
            ps = c.PS[0][:, sl * 512: sl * 512 + TT]
            rps = c.r_PS[0][sl]
            for dc in range(NDC):
                p.op("T", lambda t, ps=ps, kc=kc, dc=dc: t.matmul(
                    ps, wk[:, dc * D + kc * 128: dc * D + kc * 128 + 128],
                    xk[:, dc * TT: dc * TT + TT],
                    start=(dc == 0), stop=(dc == NDC - 1)),
                    reads=[r_wk, r_xk], writes=[rps])
            p.op("S", lambda s, ps=ps: s.activation(kr[:], ps, AF.Relu),
                 reads=[rps], writes=[r_kr])
            kasl = ka[:, kc * TT: kc * TT + TT]
            p.op("S", lambda s, kasl=kasl: s.activation(kasl, kr[:], AF.Square),
                 reads=[r_kr], writes=[r_ka])
        for tl in range(TT // 128):
            ts = T0 // 128 + tl
            sl = ts % 2
            psr = c.PS[1]
            rpsr = c.r_PS[1]
            pso = c.PS[2]
            rpso = c.r_PS[2]
            for seg in range(2):
                for dc in range(NDC):
                    p.op("T", lambda t, seg=seg, dc=dc, tl=tl: t.matmul(
                        psr[:, seg * 512:(seg + 1) * 512],
                        xr[:, dc * TT + tl * 128: dc * TT + tl * 128 + 128],
                        wr[:, dc * D + seg * 512: dc * D + (seg + 1) * 512],
                        start=(dc == 0), stop=(dc == NDC - 1)),
                        reads=[r_xr, r_wr], writes=[rpsr[seg]])
            p.op("S", lambda s, sl=sl: s.activation(rts[sl][:, 0:512], psr[:, 0:512], AF.Sigmoid),
                 reads=[rpsr[0]], writes=[r_rts[sl]])
            p.op("S", lambda s, sl=sl: s.activation(rts[sl][:, 512:1024], psr[:, 512:1024], AF.Sigmoid),
                 reads=[rpsr[1]], writes=[r_rts[sl]])
            for seg in range(2):
                for kc in range(NDC):
                    p.op("T", lambda t, seg=seg, kc=kc, tl=tl: t.matmul(
                        pso[:, seg * 512:(seg + 1) * 512],
                        ka[:, kc * TT + tl * 128: kc * TT + tl * 128 + 128],
                        wv[:, kc * D + seg * 512: kc * D + (seg + 1) * 512],
                        start=(kc == 0), stop=(kc == NDC - 1)),
                        reads=[r_ka, r_wv], writes=[rpso[seg]])
            for seg in range(2):
                p.op("V", lambda v, seg=seg, sl=sl: v.tensor_tensor(
                    tmp[:, seg * 512:(seg + 1) * 512], pso[:, seg * 512:(seg + 1) * 512],
                    rts[sl][:, seg * 512:(seg + 1) * 512], op=ALU.mult),
                    reads=[rpso[seg], r_rts[sl]], writes=[r_tmp])
            xv = c.xt[:, ts * D:(ts + 1) * D]
            p.op("V", lambda v, xv=xv: v.tensor_tensor(xv, xv, tmp[:], op=ALU.add),
                 reads=[r_tmp, c.r_x[ts]], writes=[c.r_x[ts]])



def emit_attn(c: Core, lname, wpre_at, ctxd, dnm, vd):
    k, p = c.k, c.k.p
    c.arena.reset()
    c.farena.reset()
    qb8, r_qb8 = c.smalls[lname + "_qb8"], c.r_smalls[lname + "_qb8"]
    kb, r_kb = c.smalls[lname + "_kb"], c.r_smalls[lname + "_kb"]
    bv, r_bv = c.smalls[lname + "_bv"], c.r_smalls[lname + "_bv"]
    bo, r_bo = c.smalls[lname + "_bo"], c.r_smalls[lname + "_bo"]
    cb, r_cb = c.small("cbias", [128, 16], F32)
    cbs, r_cbs = c.small("cbias_s", [1, 64], F32)
    cbb, r_cbb = c.small("cbias_b", [1, 16], BF16)

    wbig, r_wbig = c.arena.alloc(NDC * D)
    wqk, r_wqks = c.arena.alloc(2 * 2 * NDC * 128, nres=2)
    wqkb = [wqk[:, 0:2 * NDC * 128], wqk[:, 2 * NDC * 128:]]
    qc, r_qc = c.arena.alloc(S)
    kc_, r_kc = c.arena.alloc(S)
    va, r_va = c.arena.alloc(NTILE * 130)
    pt1, r_pt1 = c.arena.alloc(1024)
    pt2, r_pt2 = c.arena.alloc(1024)
    pts, r_pts = [pt1, pt2], [r_pt1, r_pt2]
    sq, r_sq = c.arena.alloc(S)
    stg, r_stg = c.arena.alloc(S)
    cin1, r_cin1 = c.arena.alloc(1024)
    cin2, r_cin2 = c.arena.alloc(1024)
    cins, r_cins = [cin1, cin2], [r_cin1, r_cin2]
    cn1, r_cn1 = c.arena.alloc(1024)
    cn2, r_cn2 = c.arena.alloc(1024)
    cns, r_cns = [cn1, cn2], [r_cn1, r_cn2]
    dnr, r_dnr = c.farena.alloc(2048)
    rb1, r_rb1 = c.farena.alloc(1024)
    rb2, r_rb2 = c.farena.alloc(1024)
    rbs, r_rbs = [rb1, rb2], [r_rb1, r_rb2]
    rc, r_rc = c.small("rcrow", [1, S], F32)
    onescol, r_onescol = c.smalls["onescol"], c.r_smalls["onescol"]

    if not hasattr(c, "r_ctxd"):
        c.r_ctxd = [Res() for _ in range(NH)]
        c.r_dnm = [Res() for _ in range(NH)]
        c.r_vd = Res()
    r_ctxd, r_dnm = c.r_ctxd, c.r_dnm

    vones = va.rearrange("p (t u s) -> p t u s", u=2, s=65)[:, :, :, 64:65]
    p.op("V", lambda v: v.memset(vones, 1.0), writes=[r_va])

    WQ, WK, WV, WO = 0, 1, 2, 3
    p.op("G", lambda g: g.dma_start(
        out=wbig.rearrange("p (dc n) -> p dc n", n=D),
        in_=wpre_at[WV].rearrange("(dc p) n -> p dc n", p=128)),
        writes=[r_wbig], dma=True)
    for ts in range(NTILE):
        psn = ts % 3
        psV = c.PS[psn]
        rV = c.r_PS[psn]
        for dc in range(NDC):
            for seg in range(2):
                p.op("T", lambda t, psV=psV, dc=dc, ts=ts, seg=seg: t.matmul(
                    psV[:, seg * 512:(seg + 1) * 512],
                    c.xln[:, dc * S + ts * 128: dc * S + (ts + 1) * 128],
                    wbig[:, dc * D + seg * 512: dc * D + (seg + 1) * 512],
                    start=(dc == 0), stop=False),
                    reads=[r_wbig, c.r_xln[dc * NTILE + ts]], writes=[rV[seg]])
        for seg in range(2):
            p.op("T", lambda t, psV=psV, seg=seg: t.matmul(
                psV[:, seg * 512:(seg + 1) * 512],
                c.ones_row[:], bv[0:1, seg * 512:(seg + 1) * 512],
                start=False, stop=True),
                reads=[c.r_ones, r_bv], writes=[rV[seg]])
        p.op("S", lambda s, psV=psV: s.copy(stg[:, 0:1024], psV[:, 0:1024]),
             reads=[rV[0], rV[1]], writes=[r_stg])
        p.op("SY", lambda s, ts=ts: s.dma_start(out=vd[ts], in_=stg[:, 0:1024]),
             reads=[r_stg], writes=[c.r_vd], dma=True)
    p.op("G", lambda g: g.dma_start(
        out=wbig.rearrange("p (dc n) -> p dc n", n=D),
        in_=wpre_at[WO].rearrange("(dc p) n -> p dc n", p=128)),
        writes=[r_wbig], dma=True)

    for g_ in range(8):
        wb, r_wb = wqkb[g_ % 2], r_wqks[g_ % 2]
        for wi, W in enumerate((WQ, WK)):
            src = wpre_at[W].rearrange("(dc p) n -> p dc n", p=128)[:, :, g_ * 128:(g_ + 1) * 128]
            dst = wb[:, wi * NDC * 128:(wi + 1) * NDC * 128]
            p.op("G", lambda g, dst=dst, src=src: g.dma_start(out=dst, in_=src),
                 writes=[r_wb], dma=True)
        wq_g = wb[:, 0:NDC * 128]
        wk_g = wb[:, NDC * 128:2 * NDC * 128]
        for u in range(2):
            vsrc = vd[:, :, g_ * 128 + u * 64: g_ * 128 + u * 64 + 64].rearrange("t p j -> p t j")
            vadst = va.rearrange("p (t u s) -> p t u s", u=2, s=65)[:, :, u, 0:64]
            p.op("SY", lambda s, vsrc=vsrc, vadst=vadst: s.dma_start(out=vadst, in_=vsrc),
                 reads=[c.r_vd], writes=[r_va], dma=True)

        for which in range(2):
            wsrc = wq_g if which == 0 else wk_g
            dst, r_dst = (qc, r_qc) if which == 0 else (kc_, r_kc)
            bias, r_bias = (qb8, r_qb8) if which == 0 else (kb, r_kb)
            for dc in range(NDC):
                for seg in range(4):
                    ps = c.PS[seg // 2][:, (seg % 2) * 512:(seg % 2) * 512 + 512]
                    rps = c.r_PS[seg // 2][seg % 2]
                    p.op("T", lambda t, ps=ps, wsrc=wsrc, dc=dc, seg=seg: t.matmul(
                        ps, wsrc[:, dc * 128:(dc + 1) * 128],
                        c.xln[:, dc * S + seg * 512: dc * S + (seg + 1) * 512],
                        start=(dc == 0), stop=(dc == NDC - 1)),
                        reads=[r_wb] + [c.r_xln[dc * NTILE + seg * 4 + j] for j in range(4)],
                        writes=[rps])
            for seg in range(4):
                ps = c.PS[seg // 2][:, (seg % 2) * 512:(seg % 2) * 512 + 512]
                rps = c.r_PS[seg // 2][seg % 2]
                p.op("S", lambda s, ps=ps, dst=dst, seg=seg, bias=bias, g_=g_: s.activation(
                    dst[:, seg * 512:(seg + 1) * 512], ps, AF.Identity,
                    bias=bias[:, g_:g_ + 1], scale=0.3535533905932738),
                    reads=[rps, r_bias], writes=[r_dst])

        for which in range(2):
            src, r_src = (qc, r_qc) if which == 0 else (kc_, r_kc)
            p.op("S", lambda s, src=src: s.activation(sq[:], src[:], AF.Square),
                 reads=[r_src], writes=[r_sq])
            for hp in range(2):
                base = hp * 64
                col0 = which * 2 + hp
                for rnd in range(4):
                    ps = c.PS[2][0:1, (rnd % 2) * 512:(rnd % 2) * 512 + 512]
                    rps = c.r_PS[2][rnd % 2]
                    p.op("T", lambda t, ps=ps, base=base, rnd=rnd: t.matmul(
                        ps, onescol[base:base + 64, 0:1],
                        sq[base:base + 64, rnd * 512:(rnd + 1) * 512],
                        start=True, stop=True),
                        reads=[r_sq, r_onescol], writes=[rps])
                    p.op("V", lambda v, ps=ps, col0=col0, rnd=rnd: v.reduce_max(
                        cbs[0:1, col0 * 8 + rnd: col0 * 8 + rnd + 1], ps, axis=AX.X),
                        reads=[rps], writes=[r_cbs])
                p.op("V", lambda v, col0=col0: v.tensor_tensor(
                    cbs[0:1, col0 * 8 + 4: col0 * 8 + 5],
                    cbs[0:1, col0 * 8 + 0: col0 * 8 + 1],
                    cbs[0:1, col0 * 8 + 1: col0 * 8 + 2], op=ALU.max),
                    reads=[r_cbs], writes=[r_cbs])
                p.op("V", lambda v, col0=col0: v.tensor_tensor(
                    cbs[0:1, col0 * 8 + 5: col0 * 8 + 6],
                    cbs[0:1, col0 * 8 + 2: col0 * 8 + 3],
                    cbs[0:1, col0 * 8 + 3: col0 * 8 + 4], op=ALU.max),
                    reads=[r_cbs], writes=[r_cbs])
                p.op("V", lambda v, col0=col0: v.tensor_tensor(
                    cbs[0:1, col0 * 8 + 2: col0 * 8 + 3],
                    cbs[0:1, col0 * 8 + 4: col0 * 8 + 5],
                    cbs[0:1, col0 * 8 + 5: col0 * 8 + 6], op=ALU.max),
                    reads=[r_cbs], writes=[r_cbs])
        for hp in range(2):
            h = 2 * g_ + hp
            qcol, kcol = (0 * 2 + hp) * 8 + 2, (1 * 2 + hp) * 8 + 2
            p.op("V", lambda v, qcol=qcol, kcol=kcol, hp=hp: v.tensor_tensor(
                cbs[0:1, 32 + hp: 33 + hp], cbs[0:1, qcol:qcol + 1],
                cbs[0:1, kcol:kcol + 1], op=ALU.add),
                reads=[r_cbs], writes=[r_cbs])
            p.op("V", lambda v, hp=hp, h=h: v.tensor_scalar(
                cbb[0:1, h:h + 1], cbs[0:1, 32 + hp:33 + hp], -0.5, None, op0=ALU.mult),
                reads=[r_cbs], writes=[r_cbb])
            ps = c.PS[2][:, 256 + hp:257 + hp]
            rps = c.r_PS[2][0]
            p.op("T", lambda t, ps=ps, h=h: t.matmul(
                ps, c.ones_row[:], cbb[0:1, h:h + 1], start=True, stop=True),
                reads=[c.r_ones, r_cbb], writes=[rps])
            p.op("V", lambda v, ps=ps, h=h: v.tensor_copy(cb[:, h:h + 1], ps),
                 reads=[rps], writes=[r_cb])

        for hp in range(2):
            h = 2 * g_ + hp
            base = hp * 64
            for qh in range(2):
                def s_ops(kt, base=base, qh=qh, h=h):
                    psS = c.PS[kt % 2]
                    rS = c.r_PS[kt % 2]
                    for seg in range(2):
                        p.op("T", lambda t, kt=kt, seg=seg, psS=psS, base=base, qh=qh: t.matmul(
                            psS[:, seg * 512:(seg + 1) * 512],
                            kc_[base:base + 64, kt * 128:(kt + 1) * 128],
                            qc[base:base + 64, qh * 1024 + seg * 512: qh * 1024 + (seg + 1) * 512],
                            start=True, stop=True),
                            reads=[r_kc, r_qc], writes=[rS[seg]])
                    p.op("S", lambda s, kt=kt, psS=psS, h=h: s.activation(
                        pts[kt % 2][:], psS[:, :], AF.Exp, bias=cb[:, h:h + 1]),
                        reads=[rS[0], rS[1], r_cb], writes=[r_pts[kt % 2]])

                def c_ops(kt, hp=hp):
                    for seg in range(2):
                        p.op("T", lambda t, kt=kt, seg=seg, hp=hp: t.matmul(
                            c.PS[2][0:65, seg * 512:(seg + 1) * 512],
                            va[:, kt * 130 + hp * 65: kt * 130 + hp * 65 + 65],
                            pts[kt % 2][:, seg * 512:(seg + 1) * 512],
                            start=(kt == 0), stop=(kt == NTILE - 1)),
                            reads=[r_va, r_pts[kt % 2]], writes=[c.r_PS[2][seg]])

                s_ops(0)
                for kt in range(NTILE):
                    if kt + 1 < NTILE:
                        s_ops(kt + 1)
                    c_ops(kt)
                p.op("V", lambda v, qh=qh: v.tensor_copy(
                    stg[0:64, qh * 1024:(qh + 1) * 1024], c.PS[2][0:64, :]),
                    reads=[c.r_PS[2][0], c.r_PS[2][1]], writes=[r_stg])
                p.op("V", lambda v, qh=qh: v.tensor_copy(
                    dnr[0:1, qh * 1024:(qh + 1) * 1024], c.PS[2][64:65, :]),
                    reads=[c.r_PS[2][0], c.r_PS[2][1]], writes=[r_dnr])
            p.op("V", lambda v: v.reciprocal(rc[0:1, :], dnr[0:1, 0:2048]),
                 reads=[r_dnr], writes=[r_rc])
            p.op("SY", lambda s, h=h: s.dma_start(out=dnm[h][None, :], in_=rc[0:1, :]),
                 reads=[r_rc], writes=[r_dnm[h]], dma=True)
            p.op("SY", lambda s, h=h: s.dma_start(out=ctxd[h], in_=stg[0:64, :]),
                 reads=[r_stg], writes=[r_ctxd[h]], dma=True)

    ctxv = ctxd.rearrange("(dc hp) r (nt j) -> hp r nt dc j", hp=2, nt=NTILE)
    dnv = dnm.rearrange("(dc hp) (nt j) -> hp nt dc j", hp=2, nt=NTILE)
    for ts in range(NTILE):
        sl = ts % 2
        for hp in range(2):
            cdst = cins[sl][64 * hp:64 * (hp + 1), :].rearrange("r (dc j) -> r dc j", j=128)
            p.op("SY", lambda s, ts=ts, sl=sl, hp=hp, cdst=cdst: s.dma_start(
                out=cdst, in_=ctxv[hp, :, ts]),
                reads=[r_ctxd[2 * dc + hp] for dc in range(NDC)],
                writes=[r_cins[sl]], dma=True)
            rdst = rbs[sl][64 * hp:64 * (hp + 1), :].rearrange("r (dc j) -> r dc j", j=128)
            p.op("SY", lambda s, ts=ts, sl=sl, hp=hp, rdst=rdst: s.dma_start(
                out=rdst, in_=dnv[hp, ts][None, :, :].broadcast_to([64, NDC, 128])),
                reads=[r_dnm[2 * dc + hp] for dc in range(NDC)],
                writes=[r_rbs[sl]], dma=True)
        p.op("V", lambda v, sl=sl: v.tensor_tensor(cns[sl][:], cins[sl][:], rbs[sl][:], op=ALU.mult),
             reads=[r_cins[sl], r_rbs[sl]], writes=[r_cns[sl]])
        psO = c.PS[sl]
        rO = c.r_PS[sl]
        for dc in range(NDC):
            for seg in range(2):
                p.op("T", lambda t, sl=sl, seg=seg, dc=dc, psO=psO: t.matmul(
                    psO[:, seg * 512:(seg + 1) * 512],
                    cns[sl][:, dc * 128:(dc + 1) * 128],
                    wbig[:, dc * D + seg * 512: dc * D + (seg + 1) * 512],
                    start=(dc == 0), stop=False),
                    reads=[r_cns[sl], r_wbig], writes=[rO[seg]])
        for seg in range(2):
            p.op("T", lambda t, sl=sl, seg=seg, psO=psO: t.matmul(
                psO[:, seg * 512:(seg + 1) * 512],
                c.ones_row[:], bo[0:1, seg * 512:(seg + 1) * 512],
                start=False, stop=True),
                reads=[c.r_ones, r_bo], writes=[rO[seg]])
        xv = c.xt[:, ts * D:(ts + 1) * D]
        p.op("V", lambda v, xv=xv, psO=psO: v.tensor_tensor(xv, xv, psO[:, :], op=ALU.add),
             reads=[rO[0], rO[1], c.r_x[ts]], writes=[c.r_x[ts]])



MOE_THIRDS = ((0, 4), (4, 8), (8, 11))


def emit_moe(c: Core, ins):
    k, p = c.k, c.k.p
    gbb, r_gbb = c.smalls["gbb"], c.r_smalls["gbb"]
    b1c, r_b1c = c.smalls["b1c"], c.r_smalls["b1c"]
    b3c, r_b3c = c.smalls["b3c"], c.r_smalls["b3c"]
    b2a, r_b2a = c.smalls["b2a"], c.r_smalls["b2a"]
    sb2r, r_sb2r = c.smalls["sb2r"], c.r_smalls["sb2r"]

    (gt, r_gt, et, r_et, sc, r_sc, bt, r_bt, sel, r_sel, gst, r_gst) = c._gate
    msk, r_msk = et, r_et
    rw, r_rw = c.small("moe_rw", [128, 128], F32)
    rwb, r_rwb = c.small("moe_rwb", [128, 128], BF16)
    nm, ssum, rs, m1, m2 = (gst[:, i * 16:(i + 1) * 16] for i in range(5))
    gt3 = gt.rearrange("p (s e) -> p s e", e=8)
    p.op("V", lambda v: v.reduce_max(nm, gt3, axis=AX.X, negate=True),
         reads=[r_gt], writes=[r_gst])
    for ts in range(NTILE):
        p.op("S", lambda s, ts=ts: s.activation(
            et[:, ts * 8:(ts + 1) * 8], gt[:, ts * 8:(ts + 1) * 8], AF.Exp,
            bias=nm[:, ts:ts + 1]),
            reads=[r_gt, r_gst], writes=[r_et])
    et3 = et.rearrange("p (s e) -> p s e", e=8)
    p.op("V", lambda v: v.reduce_sum(ssum, et3, axis=AX.X),
         reads=[r_et], writes=[r_gst])
    p.op("V", lambda v: v.reciprocal(rs, ssum), reads=[r_gst], writes=[r_gst])
    for ts in range(NTILE):
        p.op("V", lambda v, ts=ts: v.tensor_scalar(
            sc[:, ts * 8:(ts + 1) * 8], et[:, ts * 8:(ts + 1) * 8],
            rs[:, ts:ts + 1], None, op0=ALU.mult),
            reads=[r_et, r_gst], writes=[r_sc])
    sc3 = sc.rearrange("p (s e) -> p s e", e=8)
    bt3 = bt.rearrange("p (s e) -> p s e", e=8)
    p.op("V", lambda v: v.tensor_tensor(
        bt3, sc3, gbb[:, None, :].broadcast_to([128, NTILE, 8]), op=ALU.add),
        reads=[r_sc, r_gbb], writes=[r_bt])
    p.op("V", lambda v: v.reduce_max(m1, bt3, axis=AX.X), reads=[r_bt], writes=[r_gst])
    for ts in range(NTILE):
        p.op("V", lambda v, ts=ts: v.tensor_scalar(
            sel[:, ts * 8:(ts + 1) * 8], bt[:, ts * 8:(ts + 1) * 8],
            m1[:, ts:ts + 1], None, op0=ALU.is_ge),
            reads=[r_bt, r_gst], writes=[r_sel])
    p.op("V", lambda v: v.scalar_tensor_tensor(msk, sel, -1e30, bt, op0=ALU.mult, op1=ALU.add),
         reads=[r_sel, r_bt], writes=[r_msk])
    msk3 = msk.rearrange("p (s e) -> p s e", e=8)
    p.op("V", lambda v: v.reduce_max(m2, msk3, axis=AX.X), reads=[r_msk], writes=[r_gst])
    for ts in range(NTILE):
        p.op("V", lambda v, ts=ts: v.tensor_scalar(
            sel[:, ts * 8:(ts + 1) * 8], bt[:, ts * 8:(ts + 1) * 8],
            m2[:, ts:ts + 1], None, op0=ALU.is_ge),
            reads=[r_bt, r_gst], writes=[r_sel])
    p.op("V", lambda v: v.scalar_tensor_tensor(rw, sel, 0.5, sc, op0=ALU.mult, op1=ALU.mult),
         reads=[r_sel, r_sc], writes=[r_rw])
    p.op("V", lambda v: v.tensor_copy(rwb, rw), reads=[r_rw], writes=[r_rwb])

    c.arena.reset()
    c.farena.reset()
    w1s, r_w1s = c.arena.alloc(2 * NDC * 512, nres=2)
    w1b = [w1s[:, 0:NDC * 512], w1s[:, NDC * 512:]]
    w3s, r_w3s = c.arena.alloc(2 * NDC * 512, nres=2)
    w3b = [w3s[:, 0:NDC * 512], w3s[:, NDC * 512:]]
    w2s, r_w2s0 = c.arena.alloc(4 * D)
    w2b = [w2s, w2s]
    r_w2s = [r_w2s0, r_w2s0]
    aa, r_aa = c.arena.alloc(4 * S)
    rwT, r_rwT = c.arena.alloc(S)
    p1s, r_p1s = c.farena.alloc(2048)
    p3s, r_p3s = c.farena.alloc(2048)
    for ts in range(NTILE):
        tp, rtp = c.tp_slot(ts)
        p.op("T", lambda t, tp=tp, ts=ts: t.transpose(
            tp[0:8, 0:128], rwb[:, ts * 8:(ts + 1) * 8], c.ident[:]),
            reads=[r_rwb, c.r_ident], writes=[rtp])
        p.op("V", lambda v, tp=tp, ts=ts: v.tensor_copy(
            rwT[0:8, ts * 128:(ts + 1) * 128], tp[0:8, 0:128]),
            reads=[rtp], writes=[r_rwT])

    for ts in range(NTILE):
        psn = ts % 3
        psD = c.PS[psn]
        rD = c.r_PS[psn]
        for seg in range(2):
            p.op("T", lambda t, psD=psD, seg=seg, ts=ts: t.matmul(
                psD[:, seg * 512:(seg + 1) * 512],
                rwT[0:8, ts * 128:(ts + 1) * 128],
                b2a[0:8, seg * 512:(seg + 1) * 512],
                start=True, stop=True),
                reads=[r_rwT, r_b2a], writes=[rD[seg]])
        xv = c.xt[:, ts * D:(ts + 1) * D]
        p.op("V", lambda v, xv=xv, psD=psD: v.tensor_tensor(xv, xv, psD[:, :], op=ALU.add),
             reads=[rD[0], rD[1], c.r_x[ts]], writes=[c.r_x[ts]])

    third_i = 0
    for e in range(9):
        shared = (e == 8)
        w1_ap = ins["L1_ffn_sw1"] if shared else ins["L1_ffn_w1"][e]
        w3_ap = ins["L1_ffn_sw3"] if shared else ins["L1_ffn_w3"][e]
        w2_ap = ins["L1_ffn_sw2"] if shared else ins["L1_ffn_w2"][e]
        for (h0, h1) in MOE_THIRDS:
            hlen = h1 - h0
            hcols = hlen * 128
            sl = third_i % 2
            third_i += 1
            p.op("G", lambda g, sl=sl, w1_ap=w1_ap, h0=h0, h1=h1, hcols=hcols: g.dma_start(
                out=w1b[sl][:, 0:NDC * hcols].rearrange("p (dc h) -> p dc h", h=hcols),
                in_=w1_ap[:, h0 * 128:h1 * 128].rearrange("(dc p) h -> p dc h", p=128)),
                writes=[r_w1s[sl]], dma=True)
            p.op("G", lambda g, sl=sl, w3_ap=w3_ap, h0=h0, h1=h1, hcols=hcols: g.dma_start(
                out=w3b[sl][:, 0:NDC * hcols].rearrange("p (dc h) -> p dc h", h=hcols),
                in_=w3_ap[:, h0 * 128:h1 * 128].rearrange("(dc p) h -> p dc h", p=128)),
                writes=[r_w3s[sl]], dma=True)
            p.op("G", lambda g, sl=sl, w2_ap=w2_ap, h0=h0, h1=h1: g.dma_start(
                out=w2b[sl][:, 0:(h1 - h0) * D].rearrange("p (hc d) -> p hc d", d=D),
                in_=w2_ap[h0 * 128:h1 * 128, :].rearrange("(hc p) d -> p hc d", p=128)),
                writes=[r_w2s[sl]], dma=True)
            for hcl in range(hlen):
                bi = 8 * 11 + (h0 + hcl) if shared else e * 11 + (h0 + hcl)
                for dc in range(NDC):
                    for th in range(2):
                        for seg in range(2):
                            ps = c.PS[th][:, seg * 512:(seg + 1) * 512]
                            rps = c.r_PS[th][seg]
                            p.op("T", lambda t, ps=ps, sl=sl, dc=dc, hcl=hcl,
                                 th=th, seg=seg, hcols=hcols: t.matmul(
                                     ps, w1b[sl][:, dc * hcols + hcl * 128: dc * hcols + (hcl + 1) * 128],
                                     c.xln[:, dc * S + th * 1024 + seg * 512: dc * S + th * 1024 + (seg + 1) * 512],
                                     start=(dc == 0), stop=(dc == NDC - 1)),
                                reads=[r_w1s[sl]] + [c.r_xln[dc * NTILE + th * 8 + seg * 4 + j] for j in range(4)],
                                writes=[rps])
                for th in range(2):
                    p.op("S", lambda s, bi=bi, th=th: s.activation(
                        p1s[:, th * 1024:(th + 1) * 1024], c.PS[th][:, :],
                        AF.Identity, bias=b1c[:, bi:bi + 1]),
                        reads=[c.r_PS[th][0], c.r_PS[th][1], r_b1c], writes=[r_p1s])
                for th in range(2):
                    for dc in range(NDC):
                        for seg in range(2):
                            ps = c.PS[2][:, seg * 512:(seg + 1) * 512]
                            rps = c.r_PS[2][seg]
                            p.op("T", lambda t, ps=ps, sl=sl, dc=dc, hcl=hcl,
                                 th=th, seg=seg, hcols=hcols: t.matmul(
                                     ps, w3b[sl][:, dc * hcols + hcl * 128: dc * hcols + (hcl + 1) * 128],
                                     c.xln[:, dc * S + th * 1024 + seg * 512: dc * S + th * 1024 + (seg + 1) * 512],
                                     start=(dc == 0), stop=(dc == NDC - 1)),
                                reads=[r_w3s[sl]] + [c.r_xln[dc * NTILE + th * 8 + seg * 4 + j] for j in range(4)],
                                writes=[rps])
                    p.op("V", lambda v, bi=bi, th=th: v.tensor_scalar(
                        p3s[:, th * 1024:(th + 1) * 1024], c.PS[2][:, :],
                        b3c[:, bi:bi + 1], None, op0=ALU.add),
                        reads=[c.r_PS[2][0], c.r_PS[2][1], r_b3c], writes=[r_p3s])
                p.op("V", lambda v: v.tensor_tensor(p1s[:], p1s[:], p3s[:], op=ALU.mult),
                     reads=[r_p1s, r_p3s], writes=[r_p1s])
                adst = aa[:, hcl * S: (hcl + 1) * S]
                p.op("S", lambda s, adst=adst: s.activation(adst, p1s[:], AF.Silu),
                     reads=[r_p1s], writes=[r_aa])
            for ts in range(NTILE):
                psn = ts % 3
                psD = c.PS[psn]
                rD = c.r_PS[psn]
                first = True
                for seg in range(2):
                    if shared and h0 == 0:
                        p.op("T", lambda t, psD=psD, seg=seg: t.matmul(
                            psD[:, seg * 512:(seg + 1) * 512],
                            c.ones_row[:], sb2r[0:1, seg * 512:(seg + 1) * 512],
                            start=True, stop=False),
                            reads=[c.r_ones, r_sb2r], writes=[rD[seg]])
                    for hcl in range(hlen):
                        p.op("T", lambda t, psD=psD, sl=sl, seg=seg, hcl=hcl, ts=ts,
                             h0=h0, shared=shared, hlen=hlen: t.matmul(
                            psD[:, seg * 512:(seg + 1) * 512],
                            aa[:, hcl * S + ts * 128:hcl * S + (ts + 1) * 128],
                            w2b[sl][:, hcl * D + seg * 512: hcl * D + (seg + 1) * 512],
                            start=(hcl == 0 and not (shared and h0 == 0)),
                            stop=(hcl == hlen - 1)),
                            reads=[r_aa, r_w2s[sl]], writes=[rD[seg]])
                xv = c.xt[:, ts * D:(ts + 1) * D]
                scal = 0.5 if shared else None
                if shared:
                    p.op("V", lambda v, xv=xv, psD=psD: v.scalar_tensor_tensor(
                        xv, psD[:, :], 0.5, xv, op0=ALU.mult, op1=ALU.add),
                        reads=[rD[0], rD[1], c.r_x[ts]], writes=[c.r_x[ts]])
                else:
                    p.op("V", lambda v, xv=xv, psD=psD, ts=ts, e=e: v.scalar_tensor_tensor(
                        xv, psD[:, :], rw[:, ts * 8 + e: ts * 8 + e + 1], xv,
                        op0=ALU.mult, op1=ALU.add),
                        reads=[rD[0], rD[1], r_rw, c.r_x[ts]], writes=[c.r_x[ts]])




def emit_consts(c: Core):
    k, p = c.k, c.k.p
    import ml_dtypes
    ident_d = k.const("c_ident_bf",
                      np.eye(128, dtype=np.float32).astype(ml_dtypes.bfloat16))
    ones_d = k.const("c_ones_bf",
                     np.ones((1, 128), dtype=np.float32).astype(ml_dtypes.bfloat16))
    onesc_d = k.const("c_onesc_bf",
                      np.ones((128, 1), dtype=np.float32).astype(ml_dtypes.bfloat16))
    p.op("G", lambda g: g.dma_start(out=c.ident[:], in_=ident_d),
         writes=[c.r_ident], dma=True)
    p.op("G", lambda g: g.dma_start(out=c.ones_row[:], in_=ones_d),
         writes=[c.r_ones], dma=True)
    oc, r_oc = c.small("onescol", [128, 1], BF16)
    p.op("G", lambda g: g.dma_start(out=oc[:], in_=onesc_d),
         writes=[r_oc], dma=True)


def emit_prep_ffn0(c: Core, ins):
    p = c.k.p
    mur, r_mur = emit_col_prep(c, ins["L0_ffn_mu_r"], D, "mur")
    muk, r_muk = emit_col_prep(c, ins["L0_ffn_mu_k"], D, "muk")
    for (nm, mu, rmu) in (("omr", mur, r_mur), ("omk", muk, r_muk)):
        om, r_om = c.small(nm, [128, NDC], F32)
        p.op("V", lambda v, om=om, mu=mu: v.tensor_scalar(
            om[:], mu[:], 1.0, -1.0, op0=ALU.subtract, op1=ALU.mult),
            reads=[rmu], writes=[r_om])


def emit_prep_attn(c: Core, ins, lname):
    if lname == "l0":
        qb, kb, bv, bo = (ins["L0_attn_q_b"], ins["L0_attn_k_b"],
                          ins["L0_attn_v_b"], ins["L0_attn_out_b"])
    else:
        qb, kb, bv, bo = (ins["L1_attn_bq"], ins["L1_attn_bk"],
                          ins["L1_attn_bv"], ins["L1_attn_bo"])
    emit_col_prep(c, qb, D, lname + "_qb8", scale=0.3535533905932738)
    emit_col_prep(c, kb, D, lname + "_kb", scale=0.3535533905932738)
    emit_row_prep(c, bv, D, lname + "_bv")
    emit_row_prep(c, bo, D, lname + "_bo")


def emit_prep_moe(c: Core, ins):
    k, p = c.k, c.k.p
    gstg, r_gstg = c.prep_scratch("gwstg", D)
    gwhi, r_gwhi = c.small("gwhi", [128, NDC * 8], BF16)
    gwlo, r_gwlo = c.small("gwlo", [128, NDC * 8], BF16)
    identf, r_idf = c.prep_scratch("identf", 128)
    if r_idf.lw is None:
        idd = k.const("c_ident_f32", np.eye(128, dtype=np.float32))
        p.op("G", lambda g: g.dma_start(out=identf[:], in_=idd),
             writes=[r_idf], dma=True)
    p.op("G", lambda g: g.dma_start(out=gstg[0:8, :], in_=ins["L1_ffn_gate_w"]),
         writes=[r_gstg], dma=True)
    for dc in range(NDC):
        ps = c.PS[2][:, 0:128]
        rps = c.r_PS[2][0]
        p.op("T", lambda t, ps=ps, dc=dc: t.transpose(
            ps, gstg[:, dc * 128:(dc + 1) * 128], identf[:]),
            reads=[r_gstg, r_idf], writes=[rps])
        p.op("V", lambda v, ps=ps, dc=dc: v.tensor_copy(gwhi[:, dc * 8:(dc + 1) * 8], ps[:, 0:8]),
             reads=[rps], writes=[r_gwhi])
        p.op("V", lambda v, ps=ps, dc=dc: v.tensor_tensor(
            gwlo[:, dc * 8:(dc + 1) * 8], ps[:, 0:8], gwhi[:, dc * 8:(dc + 1) * 8],
            op=ALU.subtract),
            reads=[rps, r_gwhi], writes=[r_gwlo])
    gbb, r_gbb = c.small("gbb", [128, 8], F32)
    p.op("G", lambda g: g.dma_start(
        out=gbb[:], in_=ins["L1_ffn_gate_b"][None, :].broadcast_to([128, 8])),
        writes=[r_gbb], dma=True)
    b1c, r_b1c = c.small("b1c", [128, 9 * NHC], F32)
    b3c, r_b3c = c.small("b3c", [128, 9 * NHC], F32)
    for (dst, rdst, arr2, shared1) in ((b1c, r_b1c, "L1_ffn_b1", "L1_ffn_sb1"),
                                       (b3c, r_b3c, "L1_ffn_b3", "L1_ffn_sb3")):
        for e in range(9):
            src = ins[shared1] if e == 8 else ins[arr2][e]
            stage, r_stage = c.prep_scratch("colstage", 128)
            p.op("G", lambda g, src=src: g.dma_start(
                out=stage[0:NHC, :], in_=src.rearrange("(cc p) -> cc p", p=128)),
                writes=[r_stage], dma=True)
            ps = c.PS[2][:, 0:128]
            rps = c.r_PS[2][0]
            p.op("T", lambda t, ps=ps: t.transpose(ps, stage[:], identf[:]),
                 reads=[r_stage, r_idf], writes=[rps])
            p.op("V", lambda v, ps=ps, dst=dst, e=e: v.tensor_copy(
                dst[:, e * NHC:(e + 1) * NHC], ps[:, 0:NHC]),
                reads=[rps], writes=[rdst])
    b2a, r_b2a = c.small("b2a", [8, D], BF16)
    p.op("G", lambda g: g.dma_start(out=b2a[:], in_=ins["L1_ffn_b2"]),
         writes=[r_b2a], dma=True)
    emit_row_prep(c, ins["L1_ffn_sb2"], D, "sb2r")



def declare_inputs(k: K, names_shapes):
    ins = {}
    for nm, shape in names_shapes.items():
        ins[nm] = k.dram(nm, list(shape), F32, kind="ExternalInput")
    return ins


IN_SHAPES = {
    "L0_attn_q_w": (D, D), "L0_attn_q_b": (D,), "L0_attn_k_w": (D, D),
    "L0_attn_k_b": (D,), "L0_attn_v_w": (D, D), "L0_attn_v_b": (D,),
    "L0_attn_out_w": (D, D), "L0_attn_out_b": (D,),
    "L0_ffn_W_r": (D, D), "L0_ffn_W_k": (D, D), "L0_ffn_W_v": (D, D),
    "L0_ffn_mu_r": (D,), "L0_ffn_mu_k": (D,),
    "L1_attn_Wq": (D, D), "L1_attn_bq": (D,), "L1_attn_Wk": (D, D),
    "L1_attn_bk": (D,), "L1_attn_Wv": (D, D), "L1_attn_bv": (D,),
    "L1_attn_Wo": (D, D), "L1_attn_bo": (D,),
    "L1_ffn_gate_w": (NE, D), "L1_ffn_gate_b": (NE,),
    "L1_ffn_w1": (NE, D, HM), "L1_ffn_b1": (NE, HM),
    "L1_ffn_w3": (NE, D, HM), "L1_ffn_b3": (NE, HM),
    "L1_ffn_w2": (NE, HM, D), "L1_ffn_b2": (NE, D),
    "L1_ffn_sw1": (D, HM), "L1_ffn_sb1": (HM,), "L1_ffn_sw3": (D, HM),
    "L1_ffn_sb3": (HM,), "L1_ffn_sw2": (HM, D), "L1_ffn_sb2": (D,),
}

SUB_INPUTS = {
    "ffn0": ["L0_ffn_W_r", "L0_ffn_W_k", "L0_ffn_W_v", "L0_ffn_mu_r", "L0_ffn_mu_k"],
    "attn0": ["L0_attn_q_w", "L0_attn_q_b", "L0_attn_k_w", "L0_attn_k_b",
              "L0_attn_v_w", "L0_attn_v_b", "L0_attn_out_w", "L0_attn_out_b"],
    "attn1": ["L1_attn_Wq", "L1_attn_bq", "L1_attn_Wk", "L1_attn_bk",
              "L1_attn_Wv", "L1_attn_bv", "L1_attn_Wo", "L1_attn_bo"],
    "moe": ["L1_ffn_gate_w", "L1_ffn_gate_b", "L1_ffn_w1", "L1_ffn_b1",
            "L1_ffn_w3", "L1_ffn_b3", "L1_ffn_w2", "L1_ffn_b2",
            "L1_ffn_sw1", "L1_ffn_sb1", "L1_ffn_sw3", "L1_ffn_sb3",
            "L1_ffn_sw2", "L1_ffn_sb2"],
}


def build_sublayer_program(kind):
    k = K()
    x_in = k.dram("x", [B_PER_CORE, S, D], F32, kind="ExternalInput")
    y_out = k.dram("y", [B_PER_CORE, S, D], F32, kind="ExternalOutput")
    ins = declare_inputs(k, {nm: IN_SHAPES[nm] for nm in SUB_INPUTS[kind]})
    c = Core(k)
    emit_consts(c)

    if kind == "ffn0":
        emit_prep_ffn0(c, ins)
        wpre = k.dram("wpre_f0", [3, D, D], BF16)
        for i, nm in enumerate(("L0_ffn_W_r", "L0_ffn_W_k", "L0_ffn_W_v")):
            emit_prepass_matrix(c, ins[nm], wpre[i], scale=(0.5 if i == 2 else None))
        for b in range(B_PER_CORE):
            emit_load_x(c, x_in, b)
            emit_ln(c)
            emit_ffn0(c, wpre)
            emit_store_x(c, y_out, b)
    elif kind in ("attn0", "attn1"):
        lname = "l0" if kind == "attn0" else "l1"
        emit_prep_attn(c, ins, lname)
        wpre = k.dram("wpre_at", [4, D, D], BF16)
        wnames = (("L0_attn_q_w", "L0_attn_k_w", "L0_attn_v_w", "L0_attn_out_w")
                  if kind == "attn0" else
                  ("L1_attn_Wq", "L1_attn_Wk", "L1_attn_Wv", "L1_attn_Wo"))
        for i, nm in enumerate(wnames):
            emit_prepass_matrix(c, ins[nm], wpre[i])
        ctxd = k.dram("ctxd", [NH, 64, S], BF16)
        dnm = k.dram("dnm", [NH, S], F32)
        vd = k.dram("vd", [NTILE, 128, D], BF16)
        for b in range(B_PER_CORE):
            emit_load_x(c, x_in, b)
            emit_ln(c)
            emit_attn(c, lname, wpre, ctxd, dnm, vd)
            emit_store_x(c, y_out, b)
    elif kind == "moe":
        emit_prep_moe(c, ins)
        for b in range(B_PER_CORE):
            emit_load_x(c, x_in, b)
            emit_ln_gate(c, ins)
            emit_moe(c, ins)
            emit_store_x(c, y_out, b)
    k.p.emit()
    return k.nc


_PROG_CACHE = {}


def _get_prog(kind):
    if kind not in _PROG_CACHE:
        _PROG_CACHE[kind] = build_sublayer_program(kind)
    return _PROG_CACHE[kind]


def kernel(**inputs):
    x = np.ascontiguousarray(inputs["L0_attn_query"], dtype=np.float32)
    shards = np.split(x, NCORE, axis=0)
    seqs = ["ffn0", "attn0", "ffn0", "moe", "attn1", "moe"]
    cur = shards
    for kind in seqs:
        nc = _get_prog(kind)
        wmap = {nm: np.ascontiguousarray(inputs[nm], dtype=np.float32)
                for nm in SUB_INPUTS[kind]}
        in_maps = [dict(wmap, x=cur[i]) for i in range(NCORE)]
        res = run_bass_kernel_spmd(nc, in_maps, core_ids=list(range(NCORE)))
        cur = [np.asarray(r["y"]) for r in res.results]
    return np.concatenate(cur, axis=0)



def build_fused_program():
    k = K()
    x_in = k.dram("x", [B_PER_CORE, S, D], F32, kind="ExternalInput")
    y_out = k.dram("y", [B_PER_CORE, S, D], F32, kind="ExternalOutput")
    ins = declare_inputs(k, IN_SHAPES)
    c = Core(k)
    emit_consts(c)
    emit_prep_ffn0(c, ins)
    emit_prep_attn(c, ins, "l0")
    emit_prep_attn(c, ins, "l1")
    emit_prep_moe(c, ins)

    wpre_f0 = k.dram("wpre_f0", [3, D, D], BF16)
    wpre_a0 = k.dram("wpre_a0", [4, D, D], BF16)
    wpre_a1 = k.dram("wpre_a1", [4, D, D], BF16)
    for i, nm in enumerate(("L0_ffn_W_r", "L0_ffn_W_k", "L0_ffn_W_v")):
        emit_prepass_matrix(c, ins[nm], wpre_f0[i], scale=(0.5 if i == 2 else None))
    for i, nm in enumerate(("L0_attn_q_w", "L0_attn_k_w", "L0_attn_v_w", "L0_attn_out_w")):
        emit_prepass_matrix(c, ins[nm], wpre_a0[i])
    for i, nm in enumerate(("L1_attn_Wq", "L1_attn_Wk", "L1_attn_Wv", "L1_attn_Wo")):
        emit_prepass_matrix(c, ins[nm], wpre_a1[i])

    ctxd = k.dram("ctxd", [NH, 64, S], BF16)
    dnm = k.dram("dnm", [NH, S], F32)
    vd = k.dram("vd", [NTILE, 128, D], BF16)

    for b in range(B_PER_CORE):
        emit_load_x(c, x_in, b)
        emit_ln(c)
        emit_ffn0(c, wpre_f0)
        emit_ln(c)
        emit_attn(c, "l0", wpre_a0, ctxd, dnm, vd)
        emit_ln(c)
        emit_ffn0(c, wpre_f0)
        emit_ln_gate(c, ins)
        emit_moe(c, ins)
        emit_ln(c)
        emit_attn(c, "l1", wpre_a1, ctxd, dnm, vd)
        emit_ln_gate(c, ins)
        emit_moe(c, ins)
        emit_store_x(c, y_out, b)
    k.p.emit()
    return k.nc


_FUSED_CACHE = []


def kernel(**inputs):
    x = np.ascontiguousarray(inputs["L0_attn_query"], dtype=np.float32)
    shards = np.split(x, NCORE, axis=0)
    if not _FUSED_CACHE:
        _FUSED_CACHE.append(build_fused_program())
    nc = _FUSED_CACHE[0]
    wmap = {nm: np.ascontiguousarray(inputs[nm], dtype=np.float32)
            for nm in IN_SHAPES}
    in_maps = [dict(wmap, x=shards[i]) for i in range(NCORE)]
    res = run_bass_kernel_spmd(nc, in_maps, core_ids=list(range(NCORE)))
    return np.concatenate([np.asarray(r["y"]) for r in res.results], axis=0)
